# version 20
# baseline (speedup 1.0000x reference)
import numpy as np
import ml_dtypes
from contextlib import ExitStack
from collections import defaultdict
from math import sqrt

import concourse.bass as bass
import concourse.mybir as mybir
from concourse.bass_utils import run_bass_kernel_spmd

F32 = mybir.dt.float32
BF16 = mybir.dt.bfloat16
AF = mybir.ActivationFunctionType
ALU = mybir.AluOpType
BFD = ml_dtypes.bfloat16

B, S, D, H, HD = 4, 4096, 2048, 16, 128
NCORE = 8
LH = 2
SC = S // NCORE
NT = 4
DK, DV, HDK, HDV = 1024, 2048, 64, 128
USHIFT = 12.0
SCL = 1.0 / sqrt(HD)
EPS = 1e-5
NQG = S // 512


class BB:
    ENGS = ("T", "V", "S", "G", "Y")

    def __init__(self):
        self.prog = {e: [] for e in self.ENGS}
        self.semcnt = defaultdict(int)
        self.maxwait = defaultdict(int)
        self.bank_w = {}
        self.bank_r = {}

    def w(self, eng, tok):
        if tok is None:
            return
        sem, val = tok
        if val <= 0 or self.maxwait[(eng, sem)] >= val:
            return
        self.maxwait[(eng, sem)] = val
        self.prog[eng].append(("w", sem, val))

    def op(self, eng, fn, deps=(), inc=None, inc_n=1):
        for d in deps:
            self.w(eng, d)
        incs = []
        tok = None
        if inc is not None:
            self.semcnt[inc] += inc_n
            tok = (inc, self.semcnt[inc])
            incs.append((inc, inc_n))
        self.prog[eng].append(("op", fn, incs))
        return tok

    def dma(self, eng, out, in_, deps=(), inc=None):
        return self.op(eng, (lambda e, o=out, i=in_: e.dma_start(out=o, in_=i)),
                       deps, inc, inc_n=16)

    def bw_deps(self, banks):
        return [self.bank_r[b] for b in banks if self.bank_r.get(b)]

    def br_deps(self, banks):
        return [self.bank_w[b] for b in banks if self.bank_w.get(b)]

    def set_w(self, banks, tok):
        for b in banks:
            self.bank_w[b] = tok

    def set_r(self, banks, tok):
        for b in banks:
            self.bank_r[b] = tok


def mm_group(bb, banks, pairs, deps, inc="T"):
    n = len(pairs)
    alldeps = list(deps) + bb.bw_deps(banks)
    tok = None
    for i, (o, l, r) in enumerate(pairs):
        st, sp = (i == 0), (i == n - 1)
        tok = bb.op("T", (lambda e, o=o, l=l, r=r, st=st, sp=sp:
                          e.matmul(o, l, r, start=st, stop=sp)),
                    alldeps if i == 0 else (), inc if sp else None)
    bb.set_w(banks, tok)
    return tok


def build_program(do_p1=True, do_p2=True, dbg_ao=False, dbg_p2=False):
    nc = bass.Bass("TRN2", target_bir_lowering=False)

    if do_p1:
        xT = nc.dram_tensor("xT", [B, 128, 16, S], F32, kind="ExternalInput")
        wqkvT = nc.dram_tensor("wqkvT", [128, 3, LH, 16, 128], F32, kind="ExternalInput")
    if do_p2:
        w_woT = nc.dram_tensor("w_woT", [16, 128, 16, 128], F32, kind="ExternalInput")
        w_wq = nc.dram_tensor("w_wq", [8, 128, 16, 128], F32, kind="ExternalInput")
        w_wk = nc.dram_tensor("w_wk", [8, 128, 16, 128], F32, kind="ExternalInput")
        w_wv = nc.dram_tensor("w_wv", [16, 128, 16, 128], F32, kind="ExternalInput")
        w_wg = nc.dram_tensor("w_wg", [16, 128, 16, 128], F32, kind="ExternalInput")
        w_wout = nc.dram_tensor("w_wout", [16, 128, 16, 128], F32, kind="ExternalInput")
        rw_f = nc.dram_tensor("rw_f", [128, 2, 16, 16], F32, kind="ExternalInput")
        bo_f = nc.dram_tensor("bo_f", [128, 16, 1], F32, kind="ExternalInput")
        cw0_f = nc.dram_tensor("cw0_f", [128, 16, 1], F32, kind="ExternalInput")
        cw1_f = nc.dram_tensor("cw1_f", [128, 16, 1], F32, kind="ExternalInput")
        x_own = nc.dram_tensor("x_own", [NT, 128, 16, 512], F32, kind="ExternalInput")
        out_own = nc.dram_tensor("out_own", [16, 128, 2048], F32, kind="ExternalOutput")
        if dbg_p2:
            dbg = {}
            for nm, shp, dt in [("dbg_u", [128, 16, 512], BF16), ("dbg_xc", [128, 16, 512], BF16),
                                ("dbg_qrot", [128, 8, 512], BF16), ("dbg_krot", [128, 8, 512], BF16),
                                ("dbg_v", [128, 16, 512], BF16), ("dbg_g", [128, 16, 512], BF16),
                                ("dbg_rows", [16, 10, 512], F32), ("dbg_o", [128, 16, 512], F32),
                                ("dbg_ao2", [128, 16, 512], BF16),
                                ("dbg_ao4", [NT, 128, 16, 512], BF16),
                                ("dbg_u4", [NT, 128, 16, 512], BF16)]:
                dbg[nm] = nc.dram_tensor(nm, shp, dt, kind="ExternalOutput")
    c_ident = nc.dram_tensor("c_ident", [128, 128], BF16, kind="ExternalInput")
    c_ones_col = nc.dram_tensor("c_ones_col", [128, 1], BF16, kind="ExternalInput")
    c_ones_row = nc.dram_tensor("c_ones_row", [1, 128], F32, kind="ExternalInput")
    c_masks = nc.dram_tensor("c_masks", [128, 4, 512], BF16, kind="ExternalInput")
    c_rope = nc.dram_tensor("c_rope", [128, 4, 128], BF16, kind="ExternalInput")
    c_sel16 = nc.dram_tensor("c_sel16", [16, 16, 128], BF16, kind="ExternalInput")
    c_sel128 = nc.dram_tensor("c_sel128", [128, 16, 16], BF16, kind="ExternalInput")
    c_dotsel = nc.dram_tensor("c_dotsel", [128, 8, 16], BF16, kind="ExternalInput")

    full = do_p1 and do_p2
    ao_kind = "ExternalOutput" if (dbg_ao and not full) else "Internal"
    if do_p1:
        ao = nc.dram_tensor("ao", [B * LH * 128, S], BF16, kind=ao_kind)
    if do_p2:
        ag_kind = "Internal" if full else "ExternalInput"
        ag_space = "Shared" if full else "Local"
        ag = nc.dram_tensor("ag", [NCORE * B * LH * 128, S], BF16, kind=ag_kind,
                            addr_space=ag_space)
        aog = nc.dram_tensor("aog", [NCORE * B * LH * 128, 128], BF16, kind="Internal")
        cwoT = nc.dram_tensor("cwoT", [16, 128, 16, 128], BF16, kind="Internal")
        cwq = nc.dram_tensor("cwq", [8, 128, 16, 128], BF16, kind="Internal")
        cwk = nc.dram_tensor("cwk", [8, 128, 16, 128], BF16, kind="Internal")
        cwv = nc.dram_tensor("cwv", [16, 128, 16, 128], BF16, kind="Internal")
        cwg = nc.dram_tensor("cwg", [16, 128, 16, 128], BF16, kind="Internal")
        cwout = nc.dram_tensor("cwout", [16, 128, 16, 128], BF16, kind="Internal")

    bb = BB()
    ctx_all = ExitStack()
    E = ctx_all.enter_context

    id_sb = E(nc.sbuf_tensor("id_sb", [128, 128], BF16))
    onec_sb = E(nc.sbuf_tensor("onec_sb", [128, 1], BF16))
    oner_sb = E(nc.sbuf_tensor("oner_sb", [1, 128], F32))
    rope_sb = E(nc.sbuf_tensor("rope_sb", [128, 4, 128], BF16))
    sel16_sb = E(nc.sbuf_tensor("sel16_sb", [16, 16, 128], BF16))
    sel128_sb = E(nc.sbuf_tensor("sel128_sb", [128, 16, 16], BF16))
    dotsel_sb = E(nc.sbuf_tensor("dotsel_sb", [128, 8, 16], BF16))
    if do_p2:
        rw_sb = E(nc.sbuf_tensor("rw_sb", [128, 2, 16, 16], BF16))
        bo_sb = E(nc.sbuf_tensor("bo_sb", [128, 16, 1], F32))
        cw0_sb = E(nc.sbuf_tensor("cw0_sb", [128, 16, 1], F32))
        cw1_sb = E(nc.sbuf_tensor("cw1_sb", [128, 16, 1], F32))
    cpool = E(nc.sbuf_tensor("cpool", [128, 4], F32))
    tok_cp = None
    for col, val in [(0, -USHIFT), (1, SCL), (2, EPS), (3, -0.5)]:
        tok_cp = bb.op("V", (lambda e, col=col, val=val: e.memset(cpool[:, col:col + 1], val)),
                       (), inc="V")
    ap_ushift = cpool[:, 0:1]
    ap_scl = cpool[:, 1:2]
    ap_eps1 = cpool[0:1, 2:3]
    ap_eps16 = cpool[0:16, 2:3]
    ap_mhalf1 = cpool[0:1, 3:4]
    ap_mhalf16 = cpool[0:16, 3:4]
    if do_p2:
        wbuf = [E(nc.sbuf_tensor(f"wbuf{i}", [128, 16, 128], BF16)) for i in range(4)]
        xch = [E(nc.sbuf_tensor(f"xch{i}", [128, 512], F32)) for i in range(2)]

    psS0 = E(nc.psum_tensor("psS0", [128, 1024], F32))
    psS1 = E(nc.psum_tensor("psS1", [128, 1024], F32))
    psO = E(nc.psum_tensor("psO", [128, 512], F32))
    psD = E(nc.psum_tensor("psD", [1, 512], F32))
    psP0 = E(nc.psum_tensor("psP0", [128, 512], F32))
    psP1 = E(nc.psum_tensor("psP1", [128, 512], F32))

    tok_c = None
    for dst, src in [(id_sb[:, :], c_ident[:, :]), (onec_sb[:, :], c_ones_col[:, :]),
                     (oner_sb[:, :], c_ones_row[:, :]), (rope_sb[:, :, :], c_rope[:, :, :]),
                     (sel16_sb[:, :, :], c_sel16[:, :, :]), (sel128_sb[:, :, :], c_sel128[:, :, :]),
                     (dotsel_sb[:, :, :], c_dotsel[:, :, :])]:
        tok_c = bb.dma("Y", dst, src, inc="y")
    tok_rw = tok_c
    if do_p2:
        for dst, src in [(bo_sb[:, :, :], bo_f[:, :, :]), (cw0_sb[:, :, :], cw0_f[:, :, :]),
                         (cw1_sb[:, :, :], cw1_f[:, :, :])]:
            tok_c = bb.dma("Y", dst, src, inc="y")
        tok_rw = bb.dma("G", rw_sb[:, :, :, :], rw_f[:, :, :, :], inc="g")

    cache_tok = [None]
    cache_jobs = []
    cache_i = [0]
    if do_p2:
        for dstt, srct, ndo in [(cwoT, w_woT, 16), (cwq, w_wq, 8), (cwk, w_wk, 8),
                                (cwv, w_wv, 16), (cwg, w_wg, 16), (cwout, w_wout, 16)]:
            for do in range(ndo):
                cache_jobs.append((dstt[do, :, :, :], srct[do, :, :, :]))

    def drain_cache(n):
        for _ in range(n):
            if cache_i[0] >= len(cache_jobs):
                return
            dst, src = cache_jobs[cache_i[0]]
            cache_tok[0] = bb.dma("G", dst, src, inc="g")
            cache_i[0] += 1

    if do_p1:
        ctx1 = ExitStack()
        E1 = ctx1.enter_context
        wT_sb = E1(nc.sbuf_tensor("wT_sb", [128, 3, LH, 16, 128], BF16))
        xbuf = [E1(nc.sbuf_tensor(f"xbuf{i}", [128, 16, 512], BF16)) for i in range(2)]
        q_sb = [E1(nc.sbuf_tensor(f"q_sb{i}", [128, 8, 512], BF16)) for i in range(LH)]
        k_sb = [E1(nc.sbuf_tensor(f"k_sb{i}", [128, 8, 512], BF16)) for i in range(LH)]
        v_tok = [E1(nc.sbuf_tensor(f"v_tok{i}", [128, 32, 128], BF16)) for i in range(LH)]
        vtmp = [E1(nc.sbuf_tensor(f"vtmp{i}", [128, 512], BF16)) for i in range(LH)]
        masks_sb = E1(nc.sbuf_tensor("masks_sb", [128, 4, 512], BF16))
        pbuf = [E1(nc.sbuf_tensor(f"pbuf{i}", [128, 1024], BF16)) for i in range(2)]
        drow = E1(nc.sbuf_tensor("drow", [1, 512], F32))
        drec = E1(nc.sbuf_tensor("drec", [1, 512], F32))
        bcn = E1(nc.sbuf_tensor("bcn", [128, 512], F32))
        aobuf = [E1(nc.sbuf_tensor(f"aobuf{i}", [128, 512], BF16)) for i in range(2)]

        tok_masks = bb.dma("Y", masks_sb[:, :, :], c_masks[:, :, :], inc="y")
        tok_wT = bb.dma("G", wT_sb[:, :, :, :, :], wqkvT[:, :, :, :, :], inc="g")

        PB = {"P0": (psP0, ("p0",)), "P1": (psP1, ("p1",)), "O": (psO, ("o",))}
        pcyc = ["P0", "P1", "O"]

        tok_xdma = {}
        tok_xfree = {}
        ao_store = {}
        for b in range(B):
            evV = evS = None
            for tg in range(NQG):
                par = tg % 2
                fd = []
                prevkey = (b, tg - 2) if tg >= 2 else (b - 1, tg + 6)
                if prevkey in tok_xfree:
                    fd.append(tok_xfree[prevkey])
                for qq in range(4):
                    tok_xdma[(b, tg)] = bb.dma(
                        "G", xbuf[par][:, 4 * qq:4 * qq + 4, :],
                        xT[b, :, 4 * qq:4 * qq + 4, 512 * tg:512 * (tg + 1)],
                        deps=(fd + [tok_wT]) if qq == 0 else (), inc="g")
                outs6 = [(0, 0), (0, 1), (1, 0), (1, 1), (2, 0), (2, 1)]
                for oi6, (proj, lh) in enumerate(outs6):
                    psum, bank = PB[pcyc[oi6 % 3]]
                    pairs = [(psum[:, :], wT_sb[:, proj, lh, cc, :], xbuf[par][:, cc, :])
                             for cc in range(16)]
                    tokw = mm_group(bb, bank, pairs, [tok_xdma[(b, tg)]])
                    if oi6 == 5:
                        tok_xfree[(b, tg)] = tokw
                    if proj == 0:
                        dst = q_sb[lh][:, tg, :]
                    elif proj == 1:
                        dst = k_sb[lh][:, tg, :]
                    else:
                        dst = vtmp[lh][:, :]
                    if oi6 % 2 == 0:
                        tok = bb.op("V", (lambda e, d=dst, s=psum: e.tensor_copy(d, s[:, :])),
                                    [tokw], inc="V")
                        evV = tok
                    else:
                        tok = bb.op("S", (lambda e, d=dst, s=psum:
                                          e.activation(d, s[:, :], AF.Identity)),
                                    [tokw], inc="S")
                        evS = tok
                    bb.set_r(bank, tok)
                    if proj == 2:
                        fpsum, fbank = PB[pcyc[lh % 3]]
                        fdeps = [tok] + bb.bw_deps(fbank)
                        ftok = None
                        for jj in range(4):
                            ftok = bb.op("T", (lambda e, o=fpsum, src=vtmp[lh], jj=jj:
                                               e.matmul(o[:, 128 * jj:128 * (jj + 1)],
                                                        src[:, 128 * jj:128 * (jj + 1)],
                                                        id_sb[:, :], start=True, stop=True)),
                                         fdeps if jj == 0 else (), "T" if jj == 3 else None)
                        bb.set_w(fbank, ftok)
                        etok = bb.op("V", (lambda e, d=v_tok[lh][:, 4 * tg:4 * tg + 4, :], s=fpsum:
                                           e.tensor_copy(d, s[:, :].rearrange("p (a c) -> p a c", a=4))),
                                    [ftok], inc="V")
                        bb.set_r(fbank, etok)
                        evV = etok
                drain_cache(2)
            stageA_deps = [evV, evS, tok_masks, tok_c, tok_cp]

            for lh in range(LH):
                pairs = [(qg, kp2) for qg in range(NQG) for kp2 in range(2 * qg + 2)]
                stoks = {}
                cons = {}
                qg_ctok = {}
                pending = [None]

                def emit_S(gp, extra=()):
                    qg, kp2 = pairs[gp]
                    kb0, kb1 = 2 * kp2, 2 * kp2 + 1
                    Sps = psS0 if gp % 2 == 0 else psS1
                    sbanks = ("s0a", "s0b") if gp % 2 == 0 else ("s1a", "s1b")
                    deps = list(extra) + bb.bw_deps(sbanks)
                    tok = None
                    for half, kb in ((0, kb0), (1, kb1)):
                        tok = bb.op(
                            "T", (lambda e, o=Sps[:, 512 * half:512 * (half + 1)],
                                  l=k_sb[lh][:, kb // 4, 128 * (kb % 4):128 * (kb % 4) + 128],
                                  r=q_sb[lh][:, qg, :]:
                                  e.matmul(o, l, r, start=True, stop=True)),
                            deps if half == 0 else (), "T" if half == 1 else None)
                    bb.set_w(sbanks, tok)
                    stoks[gp] = tok

                def emit_norm(qg, ctok):
                    obank = "o" if qg % 2 == 0 else "p1"
                    psO_t = psO if qg % 2 == 0 else psP1
                    t2 = bb.op("V", (lambda e: e.reciprocal(drec[:, :], drow[:, :])),
                               [bb.bank_r["d"]], inc="V")
                    t3 = bb.op("T", (lambda e: e.matmul(psP0[:, :], oner_sb[:, :], drec[:, :],
                                                        start=True, stop=True)),
                               [t2] + bb.bw_deps(("p0",)), inc="T")
                    bb.set_w(("p0",), t3)
                    t4 = bb.op("V", (lambda e: e.tensor_copy(bcn[:, :], psP0[:, :])), [t3], inc="V")
                    bb.set_r(("p0",), t4)
                    abuf = aobuf[qg % 2]
                    t5deps = [t4, ctok]
                    if (b, lh, qg - 2) in ao_store:
                        t5deps.append(ao_store[(b, lh, qg - 2)])
                    elif (b, lh - 1, qg + 6) in ao_store:
                        t5deps.append(ao_store[(b, lh - 1, qg + 6)])
                    t5 = bb.op("V", (lambda e, d=abuf[:, :], s=psO_t:
                                     e.tensor_tensor(d, s[:, :], bcn[:, :], ALU.mult)),
                               t5deps, inc="V")
                    bb.set_r((obank,), t5)
                    ao_store[(b, lh, qg)] = bb.dma(
                        "Y", ao[256 * b + 128 * lh:256 * b + 128 * (lh + 1),
                                512 * qg:512 * (qg + 1)],
                        abuf[:, :], deps=[t5], inc="y")

                emit_S(0, stageA_deps)
                for gp, (qg, kp2) in enumerate(pairs):
                    npairs = 2 * qg + 2
                    kb0, kb1 = 2 * kp2, 2 * kp2 + 1
                    if gp + 1 < len(pairs):
                        emit_S(gp + 1)
                    Sps = psS0 if gp % 2 == 0 else psS1
                    sbanks = ("s0a", "s0b") if gp % 2 == 0 else ("s1a", "s1b")
                    pb = pbuf[gp % 2]
                    edeps = [stoks[gp]]
                    if (gp - 2) in cons:
                        edeps.append(cons[gp - 2])
                    etok = bb.op("S", (lambda e, d=pb[:, :], s=Sps[:, :]:
                                       e.activation(d, s, AF.Exp, bias=ap_ushift,
                                                    scale=ap_scl)),
                                 edeps, inc="S")
                    bb.set_r(sbanks, etok)
                    mtoks = []
                    for half, kb in ((0, kb0), (1, kb1)):
                        if kb >= 4 * qg:
                            j = kb - 4 * qg
                            mtoks.append(bb.op(
                                "V", (lambda e, d=pb[:, 512 * half:512 * (half + 1)],
                                      m=masks_sb[:, j, :]:
                                      e.tensor_tensor(d, d, m, ALU.mult)),
                                [etok], inc="V"))
                    obank = "o" if qg % 2 == 0 else "p1"
                    psO_t = psO if qg % 2 == 0 else psP1
                    avdeps = [etok] + mtoks
                    if kp2 == 0:
                        avdeps += bb.bw_deps((obank, "d"))
                    last = (kp2 == npairs - 1)
                    bb.op("T", (lambda e, l=v_tok[lh][:, kb0, :], r=pb[:, 0:512], o=psO_t,
                                st=(kp2 == 0): e.matmul(o[:, :], l, r, start=st, stop=False)),
                          avdeps)
                    bb.op("T", (lambda e, l=v_tok[lh][:, kb1, :], r=pb[:, 512:1024], o=psO_t,
                                sp=last: e.matmul(o[:, :], l, r, start=False, stop=sp)), ())
                    bb.op("T", (lambda e, r=pb[:, 0:512], st=(kp2 == 0):
                                e.matmul(psD[:, :], onec_sb[:, :], r, start=st, stop=False)), ())
                    ctok = bb.op("T", (lambda e, r=pb[:, 512:1024], sp=last:
                                       e.matmul(psD[:, :], onec_sb[:, :], r, start=False, stop=sp)),
                                 (), "T")
                    cons[gp] = ctok
                    if pending[0] is not None:
                        pending[0]()
                        pending[0] = None
                    if last:
                        bb.set_w((obank, "d"), ctok)
                        qg_ctok[qg] = ctok
                        t1 = bb.op("V", (lambda e: e.tensor_copy(drow[:, :], psD[:, :])),
                                   [ctok], inc="V")
                        bb.set_r(("d",), t1)
                        pending[0] = (lambda qg=qg, ctok=ctok: emit_norm(qg, ctok))
                if pending[0] is not None:
                    pending[0]()
                    pending[0] = None
                drain_cache(4)


        drain_cache(len(cache_jobs))
        ctx1.close()

    if full:
        cc_deps = [cache_tok[0], tok_rw, ("y", bb.semcnt["y"])]
        tok_cc = bb.op("G", (lambda e: e.collective_compute(
            "AllGather", ALU.bypass, replica_groups=[list(range(NCORE))],
            ins=[ao.ap().opt()], outs=[ag.ap().opt()])),
            [d for d in cc_deps if d], inc="c")
    elif do_p1:
        tok_cc = None
    else:
        drain_cache(len(cache_jobs))
        tok_cc = None

    if do_p2:
        ctx2 = ExitStack()
        E2 = ctx2.enter_context
        aov_sb = E2(nc.sbuf_tensor("aov_sb", [128, 16, 512], BF16))
        v2only = E2(nc.sbuf_tensor("v2only", [128, 16, 512], BF16))
        u_sb = E2(nc.sbuf_tensor("u_sb", [128, 16, 512], BF16))
        xc_sb = E2(nc.sbuf_tensor("xc_sb", [128, 16, 512], BF16))
        qrot = E2(nc.sbuf_tensor("qrot", [128, 8, 512], BF16))
        krot = E2(nc.sbuf_tensor("krot", [128, 8, 512], BF16))
        g_sb = E2(nc.sbuf_tensor("g_sb", [128, 16, 512], BF16))
        big4 = E2(nc.sbuf_tensor("big4", [128, 16, 512], F32))
        tmpf = [E2(nc.sbuf_tensor(f"tmpf{i}", [128, 512], F32)) for i in range(3)]
        tmpf.append(tmpf[0])
        smb = [E2(nc.sbuf_tensor(f"smb{i}", [128, 512], BF16)) for i in range(4)]
        mu_bc = E2(nc.sbuf_tensor("mu_bc", [128, 512], F32))
        rs_bc = E2(nc.sbuf_tensor("rs_bc", [128, 512], F32))
        outb = [tmpf[0], tmpf[1]]
        prodb = [E2(nc.sbuf_tensor(f"prodb{i}", [128, 128], BF16)) for i in range(2)]
        qkw_sb = E2(nc.sbuf_tensor("qkw_sb", [16, 128], BF16))
        wrow = E2(nc.sbuf_tensor("wrow", [16, 128], F32))
        wrow2 = E2(nc.sbuf_tensor("wrow2", [16, 128], F32))
        rn = ["l0r", "l1r", "s0r", "i0r", "i1r", "ms0r", "ms1r",
              "c0r", "c1r", "t0r", "t2r"]
        R = {n: E2(nc.sbuf_tensor(n, [16, 512], F32)) for n in rn}
        R["si0r"] = R["l0r"]
        R["si1r"] = R["l1r"]
        R["t1r"] = R["s0r"]
        murb = E2(nc.sbuf_tensor("murb", [16, 512], BF16))
        rsrb = E2(nc.sbuf_tensor("rsrb", [16, 512], BF16))
        srow1 = E2(nc.sbuf_tensor("srow1", [1, 512], F32))
        qrow1 = E2(nc.sbuf_tensor("qrow1", [1, 512], F32))
        murow = E2(nc.sbuf_tensor("murow", [1, 512], F32))
        varrow = qrow1
        rstdrow = qrow1
        mu2row = srow1
        hrow = {n: E2(nc.sbuf_tensor("h" + n, [16, 512], F32))
                for n in ["sum", "var", "rstd"]}
        hrow["mu"] = hrow["sum"]

        pPA, pPB = psP0, psP1
        pS1, pS2 = psD, psO[0:1, :]
        pR0, pR1 = psS1[0:16, 0:512], psS1[0:16, 512:1024]
        pDOT = psS1[0:16, 0:128]
        pBC1, pBC2 = psS0[:, 0:512], psS0[:, 512:1024]
        BK = {"PA": ("p0",), "PB": ("p1",), "S1": ("d",), "S2": ("o",),
              "R0": ("s1a",), "R1": ("s1b",), "BC1": ("s0a",), "BC2": ("s0b",)}

        pid = nc.partition_id()
        wcfg = {"wo": (cwoT, 16), "wq": (cwq, 8), "wk": (cwk, 8),
                "wv": (cwv, 16), "wg": (cwg, 16), "wout": (cwout, 16)}
        wbuf_free = {}
        wload_i = [0]

        def VOP(fn, deps):
            return bb.op("V", fn, deps, "V")

        def SOP(fn, deps):
            return bb.op("S", fn, deps, "S")

        def wload(name, do, gate):
            i = wload_i[0]
            dstt = wbuf[i % 4]
            t, _ = wcfg[name]
            dd = list(gate)
            if (i - 4) in wbuf_free:
                dd.append(wbuf_free[i - 4])
            tok = bb.dma("Y", dstt[:, :, :], t[do, :, :, :], deps=dd, inc="y")
            wload_i[0] += 1
            return i, dstt, tok

        def ln_rows(tok_stats, ndim):
            r1 = VOP((lambda e: e.tensor_copy(srow1[:, :], pS1[:, :])), [tok_stats])
            bb.set_r(BK["S1"], r1)
            r2 = VOP((lambda e: e.tensor_copy(qrow1[:, :], pS2[:, :])), [tok_stats])
            bb.set_r(BK["S2"], r2)
            r3 = VOP((lambda e: e.tensor_scalar_mul(murow[:, :], srow1[:, :], 1.0 / ndim)), [r1])
            r4 = SOP((lambda e: e.activation(mu2row[:, :], murow[:, :], AF.Square)), [r3])
            r5 = VOP((lambda e: e.tensor_scalar_mul(varrow[:, :], qrow1[:, :], 1.0 / ndim)), [r2])
            r6 = VOP((lambda e: e.tensor_tensor(varrow[:, :], varrow[:, :], mu2row[:, :],
                                                ALU.subtract)), [r5, r4])
            r7 = SOP((lambda e: e.activation(varrow[:, :], varrow[:, :], AF.Ln,
                                             bias=ap_eps1)), [r6, tok_cp])
            r8 = SOP((lambda e: e.activation(rstdrow[:, :], varrow[:, :], AF.Exp,
                                             scale=ap_mhalf1)), [r7])
            t1 = bb.op("T", (lambda e: e.matmul(pBC1[:, :], oner_sb[:, :], murow[:, :],
                                                start=True, stop=True)),
                       [r3] + bb.bw_deps(BK["BC1"]), "T")
            bb.set_w(BK["BC1"], t1)
            t2 = bb.op("T", (lambda e: e.matmul(pBC2[:, :], oner_sb[:, :], rstdrow[:, :],
                                                start=True, stop=True)),
                       [r8] + bb.bw_deps(BK["BC2"]), "T")
            bb.set_w(BK["BC2"], t2)
            c1 = VOP((lambda e: e.tensor_copy(mu_bc[:, :], pBC1[:, :])), [t1])
            bb.set_r(BK["BC1"], c1)
            c2 = VOP((lambda e: e.tensor_copy(rs_bc[:, :], pBC2[:, :])), [t2])
            bb.set_r(BK["BC2"], c2)
            return c1, c2

        cache_gate = [cache_tok[0], tok_rw] if cache_tok[0] else []
        for t in range(NT):
            gate = ([tok_cc] if (tok_cc and t == 0) else [])
            if t == 0:
                gate += cache_gate
            gate = [g for g in gate if g]
            wgate = cache_gate if t == 0 else []



            def emit_ao_prefetch(tt, extra):
                def mk_bulk(e, tt=tt):
                    off = e.snap(pid * SC + 128 * tt, min_val=0, max_val=S - 128)
                    return e.dma_start(out=aog[:, :], in_=ag[:, bass.ds(off, 128)])
                btok = bb.op("G", mk_bulk, extra, inc="g", inc_n=16)
                atok = None
                aogv = aog.rearrange("(r b f) s -> f r b s", r=NCORE, b=B)
                for cc in range(16):
                    atok = bb.dma("G", aov_sb[:, cc, :].rearrange("p (a c) -> p a c", a=4),
                                  aogv[128 * (cc % 2):128 * (cc % 2) + 128, cc // 2, :, :],
                                  deps=[btok] if cc == 0 else (), inc="g")
                return atok

            if not hasattr(bb, "ao_pref"):
                bb.ao_pref = {}
            if t in bb.ao_pref:
                ao_tok = bb.ao_pref[t]
            else:
                ao_tok = emit_ao_prefetch(t, gate)

            if dbg_p2 and t == 0:
                bb.dma("Y", dbg["dbg_ao2"][:, :, :], aov_sb[:, :, :], deps=[ao_tok], inc="y")
            if dbg_p2:
                bb.dma("G", dbg["dbg_ao4"][t, :, :, :], aov_sb[:, :, :], deps=[ao_tok], inc="g")
            prev_evac = {}
            stat_toks = {}
            jobs = {}

            def emit_statsA(do, first, last):
                e2, e3, sq = jobs[do]
                d1 = [e2] + (bb.bw_deps(BK["S1"]) if first else [])
                bb.op("T", (lambda e, do=do, st=first, sp=last:
                            e.matmul(pS1[:, :], onec_sb[:, :], u_sb[:, do, :], start=st, stop=sp)),
                      d1, "T" if last else None)
                d2 = [e3] + (bb.bw_deps(BK["S2"]) if first else [])
                tk = bb.op("T", (lambda e, sq=sq, st=first, sp=last:
                                 e.matmul(pS2[:, :], onec_sb[:, :], sq[:, :], start=st, stop=sp)),
                           d2, "T")
                stat_toks[do] = tk
                return tk

            for do in range(16):
                wi, wt, wtok = wload("wo", do, wgate)
                xdeps = [prev_evac[do - 2]] if (do - 2) in prev_evac else \
                    list(getattr(bb, "last_xch", []))
                xtok = bb.dma("Y", xch[do % 2][:, :], x_own[t, :, do, :],
                              deps=xdeps, inc="y")
                ps = pPA if do % 2 == 0 else pPB
                bank = BK["PA"] if do % 2 == 0 else BK["PB"]
                pairs = [(ps[:, :], wt[:, cc, :], aov_sb[:, cc, :]) for cc in range(16)]
                mtok = mm_group(bb, bank, pairs, [wtok, ao_tok] + gate)
                wbuf_free[wi] = mtok
                if do == 15:
                    bb.wo_done = mtok
                    if t + 1 < NT:
                        bb.ao_pref[t + 1] = emit_ao_prefetch(t + 1, [mtok])
                tf = tmpf[do % 2]
                e1deps = [mtok]
                if (do - 2) in prev_evac:
                    e1deps.append(prev_evac[do - 2])
                elif hasattr(bb, "last_out2"):
                    e1deps.append(bb.last_out2[do % 2])
                e1 = SOP((lambda e, d=tf[:, :], s=ps, bias=bo_sb[:, do, :]:
                          e.activation(d, s[:, :], AF.Identity, bias=bias)), e1deps)
                bb.set_r(bank, e1)
                e2 = VOP((lambda e, do=do, a=tf[:, :], x=xch[do % 2][:, :]:
                          e.tensor_tensor(u_sb[:, do, :], a, x, ALU.add)), [e1, xtok])
                prev_evac[do] = e2
                sq = smb[do % 2]
                e3deps = [e2]
                if (do - 2) in stat_toks:
                    e3deps.append(stat_toks[do - 2])
                e3 = SOP((lambda e, d=sq[:, :], do=do: e.activation(d, u_sb[:, do, :], AF.Square)),
                         e3deps)
                jobs[do] = (e2, e3, sq)
                if do >= 1:
                    emit_statsA(do - 1, do - 1 == 0, False)
            tok_stats = emit_statsA(15, False, True)
            bb.set_w(BK["S1"], tok_stats)
            bb.set_w(BK["S2"], tok_stats)
            bb.last_xch = [prev_evac[14], prev_evac[15]]

            c1, c2 = ln_rows(tok_stats, 2048.0)
            u_toks = {}
            lf = getattr(bb, "last_fin", None)
            for do in range(16):
                eng = "V" if do % 3 else "G"
                sem = "V" if eng == "V" else "P"
                a1 = bb.op(eng, (lambda e, do=do: e.tensor_tensor(
                    big4[:, do, :], u_sb[:, do, :], mu_bc[:, :], ALU.subtract)),
                    [c1, c2] + ([lf] if (lf and eng == "G") else []), sem)
                u_toks[do] = bb.op(eng, (lambda e, do=do: e.tensor_tensor(
                    u_sb[:, do, :], big4[:, do, :], rs_bc[:, :], ALU.mult)), [a1], sem)

            if dbg_p2 and t == 0:
                bb.dma("Y", dbg["dbg_u"][:, :, :], u_sb[:, :, :],
                       deps=list(u_toks.values()), inc="y")
            if dbg_p2:
                bb.dma("Y", dbg["dbg_u4"][t, :, :, :], u_sb[:, :, :],
                       deps=list(u_toks.values()), inc="y")
            xc_toks = {}
            for do in range(16):
                eng = "V"
                sem = "V"
                m1 = bb.op(eng, (lambda e, do=do: e.tensor_scalar_mul(
                    big4[:, do, 0:128], u_sb[:, do, 0:128], cw1_sb[:, do, :])),
                    [u_toks[do]], sem)
                m2 = bb.op(eng, (lambda e, do=do: e.tensor_scalar_mul(
                    big4[:, do, 128:512], u_sb[:, do, 128:512], cw1_sb[:, do, :])),
                    [m1], sem)
                m3 = bb.op(eng, (lambda e, do=do: e.scalar_tensor_tensor(
                    big4[:, do, 128:512], u_sb[:, do, 0:384], cw0_sb[:, do, :],
                    big4[:, do, 128:512], ALU.mult, ALU.add)), [m2], sem)
                xc_toks[do] = SOP((lambda e, do=do: e.activation(xc_sb[:, do, :],
                                                                big4[:, do, :], AF.Silu)), [m3])
            xc_all = list(xc_toks.values())

            if dbg_p2 and t == 0:
                bb.dma("Y", dbg["dbg_xc"][:, :, :], xc_sb[:, :, :], deps=xc_all, inc="y")
            def proj(name, ndo, rhs_sb, evac_cb, extra_first_deps=()):
                toks = []
                for do in range(ndo):
                    wi, wt, wtok = wload(name, do, wgate)
                    ps = pPA if do % 2 == 0 else pPB
                    bank = BK["PA"] if do % 2 == 0 else BK["PB"]
                    deps = [wtok] + (list(extra_first_deps) if do == 0 else [])
                    pairs = [(ps[:, :], wt[:, cc, :], rhs_sb[:, cc, :]) for cc in range(16)]
                    mtok = mm_group(bb, bank, pairs, deps)
                    wbuf_free[wi] = mtok
                    toks.append(evac_cb(do, mtok, ps, bank))
                return toks

            def rope_evac(dst_rot, act):
                def cb(do, mtok, ps, bank):
                    pre = smb[2 + do % 2]
                    e1deps = [mtok]
                    if hasattr(cb, "prev2") and (do - 2) in cb.prev2:
                        e1deps.append(cb.prev2[do - 2])
                    e1 = SOP((lambda e, d=pre[:, :], s=ps: e.activation(d, s[:, :], act)), e1deps)
                    bb.set_r(bank, e1)
                    rps = pBC1 if do % 2 == 0 else pBC2
                    rbank = BK["BC1"] if do % 2 == 0 else BK["BC2"]
                    rdeps = [e1] + bb.bw_deps(rbank)
                    rtok = None
                    for bbk in range(4):
                        rtok = bb.op("T", (lambda e, o=rps, pre=pre, bbk=bbk:
                                           e.matmul(o[:, 128 * bbk:128 * (bbk + 1)],
                                                    rope_sb[:, bbk, :],
                                                    pre[:, 128 * bbk:128 * (bbk + 1)],
                                                    start=True, stop=True)),
                                     rdeps if bbk == 0 else (), "T" if bbk == 3 else None)
                    bb.set_w(rbank, rtok)
                    ev = VOP((lambda e, d=dst_rot[:, do, :], s=rps: e.tensor_copy(d, s)), [rtok])
                    bb.set_r(rbank, ev)
                    cb.prev2[do] = rtok
                    return ev
                cb.prev2 = {}
                return cb

            q_toks = proj("wq", 8, xc_sb, rope_evac(qrot, AF.Silu), xc_all)
            k_toks = proj("wk", 8, xc_sb, rope_evac(krot, AF.Identity))

            def v_evac(do, mtok, ps, bank):
                ev = VOP((lambda e, do=do, s=ps: e.tensor_copy(v2only[:, do, :], s[:, :])), [mtok])
                bb.set_r(bank, ev)
                return ev

            v_toks = proj("wv", 16, xc_sb, v_evac)
            v2_sb = v2only

            def g_evac(do, mtok, ps, bank):
                ev = SOP((lambda e, do=do, s=ps: e.activation(g_sb[:, do, :], s[:, :], AF.Silu)),
                         [mtok])
                bb.set_r(bank, ev)
                return ev

            if dbg_p2 and t == 0:
                bb.dma("Y", dbg["dbg_qrot"][:, :, :], qrot[:, :, :], deps=q_toks, inc="y")
                bb.dma("Y", dbg["dbg_krot"][:, :, :], krot[:, :, :], deps=k_toks, inc="y")
                bb.dma("Y", dbg["dbg_v"][:, :, :], v2only[:, :, :], deps=v_toks, inc="y")
            rtokA = None
            d0 = list(v_toks) + [tok_rw] + bb.bw_deps(BK["R0"])
            for cc in range(16):
                rtokA = bb.op("T", (lambda e, cc=cc: e.matmul(pR0[:, :], rw_sb[:, 0, cc, :],
                                                              v2_sb[:, cc, :],
                                                              start=(cc == 0), stop=(cc == 15))),
                              d0 if cc == 0 else (), "T" if cc == 15 else None)
            bb.set_w(BK["R0"], rtokA)
            rtokB = None
            d1_ = bb.bw_deps(BK["R1"])
            for cc in range(16):
                rtokB = bb.op("T", (lambda e, cc=cc: e.matmul(pR1[:, :], rw_sb[:, 1, cc, :],
                                                              v2_sb[:, cc, :],
                                                              start=(cc == 0), stop=(cc == 15))),
                              d1_ if cc == 0 else (), "T" if cc == 15 else None)
            bb.set_w(BK["R1"], rtokB)
            rA = VOP((lambda e: e.tensor_copy(R["l0r"][:, :], pR0[:, :])), [rtokA])
            bb.set_r(BK["R0"], rA)
            rB = VOP((lambda e: e.tensor_copy(R["l1r"][:, :], pR1[:, :])), [rtokB])
            bb.set_r(BK["R1"], rB)

            h1 = VOP((lambda e: e.tensor_tensor(R["t0r"][:, :], R["l0r"][:, :], R["l1r"][:, :],
                                                ALU.subtract)), [rA, rB])
            h2 = SOP((lambda e: e.activation(R["s0r"][:, :], R["t0r"][:, :], AF.Sigmoid)), [h1])
            h3 = VOP((lambda e: e.tensor_tensor(R["i0r"][:, :], R["l0r"][:, :], R["l1r"][:, :],
                                                ALU.is_ge)), [rA, rB])
            h4 = VOP((lambda e: e.tensor_scalar(R["i1r"][:, :], R["i0r"][:, :], -1.0, 1.0,
                                                ALU.mult, ALU.add)), [h3])
            h5 = VOP((lambda e: e.tensor_tensor(R["ms0r"][:, :], R["s0r"][:, :], R["i0r"][:, :],
                                                ALU.mult)), [h2, h3])
            h6 = VOP((lambda e: e.tensor_scalar(R["t1r"][:, :], R["s0r"][:, :], -1.0, 1.0,
                                                ALU.mult, ALU.add)), [h2])
            h7 = VOP((lambda e: e.tensor_tensor(R["ms1r"][:, :], R["t1r"][:, :], R["i1r"][:, :],
                                                ALU.mult)), [h6, h4])
            cs0 = VOP((lambda e: e.tensor_copy(R["si0r"][:, 0:128], R["i0r"][:, 0:128])), [h3])
            cs1 = VOP((lambda e: e.tensor_copy(R["si1r"][:, 0:128], R["i1r"][:, 0:128])), [h4])
            for blk in range(1, 4):
                cs0 = VOP((lambda e, blk=blk: e.tensor_tensor(
                    R["si0r"][:, 128 * blk:128 * (blk + 1)],
                    R["si0r"][:, 128 * (blk - 1):128 * blk],
                    R["i0r"][:, 128 * blk:128 * (blk + 1)], ALU.add)), [cs0])
                cs1 = VOP((lambda e, blk=blk: e.tensor_tensor(
                    R["si1r"][:, 128 * blk:128 * (blk + 1)],
                    R["si1r"][:, 128 * (blk - 1):128 * blk],
                    R["i1r"][:, 128 * blk:128 * (blk + 1)], ALU.add)), [cs1])
            HKS = HDK ** (-0.5)
            n1 = VOP((lambda e: e.tensor_scalar(R["t0r"][:, :], R["si0r"][:, :], 1.0, HKS,
                                                ALU.min, ALU.mult)), [cs0])
            n2 = VOP((lambda e: e.tensor_scalar_max(R["t1r"][:, :], R["si0r"][:, :], 1.0)), [n1])
            n3 = VOP((lambda e: e.reciprocal(R["t2r"][:, :], R["t1r"][:, :])), [n2])
            n4 = VOP((lambda e: e.tensor_tensor(R["c0r"][:, :], R["ms0r"][:, :], R["t0r"][:, :],
                                                ALU.mult)), [n3, h5])
            n5 = VOP((lambda e: e.tensor_tensor(R["c0r"][:, :], R["c0r"][:, :], R["t2r"][:, :],
                                                ALU.mult)), [n4])
            p1_ = VOP((lambda e: e.tensor_scalar(R["t0r"][:, :], R["si1r"][:, :], 1.0, HKS,
                                                 ALU.min, ALU.mult)), [cs1, n5])
            p2_ = VOP((lambda e: e.tensor_scalar_max(R["t1r"][:, :], R["si1r"][:, :], 1.0)), [p1_])
            p3_ = VOP((lambda e: e.reciprocal(R["t2r"][:, :], R["t1r"][:, :])), [p2_])
            p4_ = VOP((lambda e: e.tensor_tensor(R["c1r"][:, :], R["ms1r"][:, :], R["t0r"][:, :],
                                                 ALU.mult)), [p3_, h7])
            row_tok = VOP((lambda e: e.tensor_tensor(R["c1r"][:, :], R["c1r"][:, :], R["t2r"][:, :],
                                                     ALU.mult)), [p4_])

            if dbg_p2 and t == 0:
                for ri, rnm in enumerate(["l0r", "l1r", "s0r", "i0r", "i1r", "ms0r",
                                          "ms1r", "si0r", "c0r", "c1r"]):
                    bb.dma("Y", dbg["dbg_rows"][:, ri, :], R[rnm][:, :], deps=[row_tok], inc="y")
            o_tok = {}
            prev_dtok = None
            prev_btok = None
            lm_pairs = [(l, m) for l in range(4) for m in range(l + 1)]

            def pair_seq(l, m):
                    nonlocal prev_dtok, prev_btok
                    ddeps = [row_tok] + q_toks + k_toks + bb.bw_deps(("s1a",))
                    dtok = None
                    ptoks = {}
                    for c8 in range(8):
                        pr = prodb[c8 % 2]
                        pdeps = []
                        if (c8 - 2) in ptoks:
                            pdeps.append(ptoks[c8 - 2])
                        elif prev_dtok is not None:
                            pdeps.append(prev_dtok)
                        pt = VOP((lambda e, pr=pr, c8=c8, l=l, m=m:
                                  e.tensor_tensor(pr[:, :], qrot[:, c8, 128 * l:128 * (l + 1)],
                                                  krot[:, c8, 128 * m:128 * (m + 1)], ALU.mult)),
                                 pdeps)
                        dtok = bb.op("T", (lambda e, pr=pr, c8=c8: e.matmul(
                            pDOT, dotsel_sb[:, c8, :], pr[:, :],
                            start=(c8 == 0), stop=(c8 == 7))),
                            (ddeps + [pt]) if c8 == 0 else [pt], "T")
                        ptoks[c8] = dtok
                    prev_dtok = dtok
                    bb.set_w(("s1a",), dtok)
                    w1 = VOP((lambda e, l=l, m=m: e.tensor_tensor(
                        wrow[:, :], R["c0r"][:, 128 * l:128 * (l + 1)],
                        R["i0r"][:, 128 * m:128 * (m + 1)], ALU.mult)),
                        [row_tok] + ([prev_btok] if prev_btok else []))
                    w2 = VOP((lambda e, l=l, m=m: e.tensor_tensor(
                        wrow2[:, :], R["c1r"][:, 128 * l:128 * (l + 1)],
                        R["i1r"][:, 128 * m:128 * (m + 1)], ALU.mult)), [w1])
                    w3 = VOP((lambda e: e.tensor_tensor(wrow[:, :], wrow[:, :], wrow2[:, :],
                                                        ALU.add)), [w2])
                    qk1 = VOP((lambda e: e.tensor_tensor(qkw_sb[:, :], pDOT, wrow[:, :],
                                                         ALU.mult)), [w3, dtok])
                    bb.set_r(("s1a",), qk1)
                    for q in range(4):
                        bps = pBC1 if q % 2 == 0 else pBC2
                        bbank = BK["BC1"] if q % 2 == 0 else BK["BC2"]
                        bdeps = [qk1] + bb.bw_deps(bbank)
                        btok = None
                        for j in range(4):
                            cc = 4 * q + j
                            btok = bb.op("T", (lambda e, o=bps, j=j, cc=cc: e.matmul(
                                o[:, 128 * j:128 * (j + 1)], sel16_sb[:, cc, :],
                                qkw_sb[:, :], start=True, stop=True)),
                                bdeps if j == 0 else (), "T" if j == 3 else None)
                        bb.set_w(bbank, btok)
                        bview = bps.rearrange("p (a c) -> p a c", a=4)
                        v2q = v2_sb[:, 4 * q:4 * q + 4, 128 * m:128 * (m + 1)]
                        b4q = big4[:, 4 * q:4 * q + 4, 128 * l:128 * (l + 1)]
                        vdeps = [btok] + ([v_toks[4 * q + 3]] if m == 0 and l == 0 else [])
                        if q in (1, 3) and hasattr(bb, "last_out2"):
                            vdeps.append(bb.last_out2[0])
                        if m == 0 and l == 0 and getattr(bb, "last_fin", None):
                            vdeps.append(bb.last_fin)
                        if m == 0:
                            ot = VOP((lambda e, bv=bview, vq=v2q, oq=b4q:
                                      e.tensor_tensor(oq, bv, vq, ALU.mult)),
                                     vdeps + ([o_tok[(l, q)]] if (l, q) in o_tok else []))
                        else:
                            tq = tmpf[2] if q % 2 == 0 else tmpf[0]
                            tv = tq[:, :].rearrange("p (a c) -> p a c", a=4)
                            ot1 = VOP((lambda e, bv=bview, vq=v2q, tv=tv:
                                       e.tensor_tensor(tv, bv, vq, ALU.mult)), vdeps)
                            ot = bb.op("V", (lambda e, tv=tv, oq=b4q:
                                             e.tensor_tensor(oq, oq, tv, ALU.add)),
                                       [ot1, o_tok[(l, q)]], "V")
                        bb.set_r(bbank, ot)
                        o_tok[(l, q)] = ot
                        prev_btok = btok

            g_toks = []
            for do in range(16):
                wi, wt, wtok = wload("wg", do, wgate)
                psx = pPA if do % 2 == 0 else pPB
                bankx = BK["PA"] if do % 2 == 0 else BK["PB"]
                pairs_ = [(psx[:, :], wt[:, cc, :], xc_sb[:, cc, :]) for cc in range(16)]
                mtok = mm_group(bb, bankx, pairs_, [wtok])
                wbuf_free[wi] = mtok
                g_toks.append(g_evac(do, mtok, psx, bankx))
                if do < len(lm_pairs):
                    pair_seq(*lm_pairs[do])
            if dbg_p2 and t == 0:
                bb.dma("Y", dbg["dbg_g"][:, :, :], g_sb[:, :, :], deps=g_toks, inc="y")

            ojobs = {}
            ostat = {}

            def emit_statsO(cc, first, last):
                s1, s2, ob, sq = ojobs[cc]
                d1 = [s1] + (bb.bw_deps(BK["R0"]) if first else [])
                bb.op("T", (lambda e, ob=ob, cc=cc, st=first, sp=last: e.matmul(
                    pR0[:, :], sel128_sb[:, cc, :], ob[:, :], start=st, stop=sp)),
                    d1, "T" if last else None)
                d2 = [s2] + (bb.bw_deps(BK["R1"]) if first else [])
                tk = bb.op("T", (lambda e, sq=sq, cc=cc, st=first, sp=last: e.matmul(
                    pR1[:, :], sel128_sb[:, cc, :], sq[:, :], start=st, stop=sp)),
                    d2, "T")
                ostat[cc] = tk
                return tk

            for cc in range(16):
                q = cc // 4
                deps = [o_tok[(3, q)], o_tok[(2, q)], o_tok[(1, q)], o_tok[(0, q)]]
                ob = smb[cc % 2]
                sq = smb[2 + cc % 2]
                sdep = list(deps)
                if (cc - 2) in ostat:
                    sdep.append(ostat[cc - 2])
                s1 = SOP((lambda e, d=ob[:, :], cc=cc: e.activation(d, big4[:, cc, :],
                                                                    AF.Identity)), sdep)
                s2 = SOP((lambda e, d=sq[:, :], cc=cc: e.activation(d, big4[:, cc, :],
                                                                    AF.Square)), [s1])
                ojobs[cc] = (s1, s2, ob, sq)
                if cc >= 1:
                    emit_statsO(cc - 1, cc - 1 == 0, False)
            htok = emit_statsO(15, False, True)
            bb.set_w(BK["R0"], htok)
            bb.set_w(BK["R1"], htok)
            hr1 = VOP((lambda e: e.tensor_copy(hrow["sum"][:, :], pR0[:, :])), [htok])
            bb.set_r(BK["R0"], hr1)
            hr2 = VOP((lambda e: e.tensor_scalar_mul(hrow["rstd"][:, :], pR1[:, :], 1.0 / HDV)),
                      [htok])
            bb.set_r(BK["R1"], hr2)
            hr3 = VOP((lambda e: e.tensor_scalar_mul(hrow["mu"][:, :], hrow["sum"][:, :],
                                                     1.0 / HDV)), [hr1])
            hr4 = SOP((lambda e: e.activation(hrow["var"][:, :], hrow["mu"][:, :], AF.Square)),
                      [hr3])
            hr6 = VOP((lambda e: e.tensor_tensor(hrow["var"][:, :], hrow["rstd"][:, :],
                                                 hrow["var"][:, :], ALU.subtract)), [hr2, hr4])
            hr7 = SOP((lambda e: e.activation(hrow["var"][:, :], hrow["var"][:, :], AF.Ln,
                                              bias=ap_eps16)), [hr6, tok_cp])
            hr8 = SOP((lambda e: e.activation(hrow["rstd"][:, :], hrow["var"][:, :], AF.Exp,
                                              scale=ap_mhalf16)), [hr7])
            hb1 = SOP((lambda e: e.activation(murb[:, :], hrow["mu"][:, :], AF.Identity)), [hr3])
            hb2 = SOP((lambda e: e.activation(rsrb[:, :], hrow["rstd"][:, :], AF.Identity)), [hr8])
            go_toks = []
            for cc in range(16):
                bd1 = [hb1] + bb.bw_deps(BK["BC1"])
                tb1 = bb.op("T", (lambda e, cc=cc: e.matmul(pBC1[:, :], sel16_sb[:, cc, :],
                                                            murb[:, :], start=True, stop=True)),
                            bd1, "T")
                bb.set_w(BK["BC1"], tb1)
                bd2 = [hb2] + bb.bw_deps(BK["BC2"])
                tb2 = bb.op("T", (lambda e, cc=cc: e.matmul(pBC2[:, :], sel16_sb[:, cc, :],
                                                            rsrb[:, :], start=True, stop=True)),
                            bd2, "T")
                bb.set_w(BK["BC2"], tb2)
                g1 = VOP((lambda e, cc=cc: e.tensor_tensor(tmpf[0][:, :], big4[:, cc, :],
                                                           pBC1[:, :], ALU.subtract)), [tb1])
                bb.set_r(BK["BC1"], g1)
                g2 = VOP((lambda e, cc=cc: e.tensor_tensor(tmpf[1][:, :], tmpf[0][:, :],
                                                           pBC2[:, :], ALU.mult)), [g1, tb2])
                bb.set_r(BK["BC2"], g2)
                g3 = VOP((lambda e, cc=cc: e.tensor_tensor(g_sb[:, cc, :], tmpf[1][:, :],
                                                           g_sb[:, cc, :], ALU.mult)),
                         [g2, g_toks[cc]])
                go_toks.append(g3)
            go_sb = g_sb

            dbg_o_tok = None
            if dbg_p2 and t == 0:
                dbg_o_tok = bb.dma("Y", dbg["dbg_o"][:, :, :], big4[:, :, :],
                                   deps=[o_tok[(l_, q_)] for l_ in range(4) for q_ in range(4)],
                                   inc="y")
            fjobs = {}
            fstat = {}

            def emit_statsF(do, first, last):
                f2, f3, xb, sqb = fjobs[do]
                d1 = [f2] + (bb.bw_deps(BK["S1"]) if first else [])
                bb.op("T", (lambda e, xb=xb, st=first, sp=last: e.matmul(
                    pS1[:, :], onec_sb[:, :], xb[:, :], start=st, stop=sp)),
                    d1, "T" if last else None)
                d2 = [f3] + (bb.bw_deps(BK["S2"]) if first else [])
                tk = bb.op("T", (lambda e, sqb=sqb, st=first, sp=last: e.matmul(
                    pS2[:, :], onec_sb[:, :], sqb[:, :], start=st, stop=sp)),
                    d2, "T")
                fstat[do] = tk
                return tk

            for do in range(16):
                wi, wt, wtok = wload("wout", do, wgate)
                ps = pPA if do % 2 == 0 else pPB
                bank = BK["PA"] if do % 2 == 0 else BK["PB"]
                pairs = [(ps[:, :], wt[:, cc, :], go_sb[:, cc, :]) for cc in range(16)]
                mtok = mm_group(bb, bank, pairs, [wtok] + (go_toks if do == 0 else []))
                wbuf_free[wi] = mtok
                f1 = VOP((lambda e, do=do, s=ps: e.tensor_tensor(big4[:, do, :], s[:, :],
                                                                 u_sb[:, do, :], ALU.add)),
                         [mtok, u_toks[do]] + ([dbg_o_tok] if dbg_o_tok else []))
                bb.set_r(bank, f1)
                xb = smb[do % 2]
                sqb = smb[2 + do % 2]
                f2deps = [f1]
                if (do - 2) in fstat:
                    f2deps.append(fstat[do - 2])
                f2 = SOP((lambda e, d=xb[:, :], do=do: e.activation(d, big4[:, do, :],
                                                                    AF.Identity)), f2deps)
                f3 = SOP((lambda e, d=sqb[:, :], do=do: e.activation(d, big4[:, do, :],
                                                                     AF.Square)), [f2])
                fjobs[do] = (f2, f3, xb, sqb)
                if do >= 1:
                    emit_statsF(do - 1, do - 1 == 0, False)
            ftok = emit_statsF(15, False, True)
            bb.set_w(BK["S1"], ftok)
            bb.set_w(BK["S2"], ftok)
            fc1, fc2 = ln_rows(ftok, 2048.0)
            if not hasattr(bb, "out_store"):
                bb.out_store = {}
            for do in range(16):
                zeng = "V" if do % 3 else "G"
                zsem = "V" if zeng == "V" else "P"
                zt = tmpf[2] if zeng == "V" else big4[:, do, :]
                z1 = bb.op(zeng, (lambda e, do=do, zt=zt: e.tensor_tensor(
                    zt[:, :] if zt is tmpf[2] else zt, big4[:, do, :],
                    mu_bc[:, :], ALU.subtract)), [fc1, fc2], zsem)
                ob = outb[do % 2]
                z2deps = [z1]
                if (t, do - 2) in bb.out_store:
                    z2deps.append(bb.out_store[(t, do - 2)])
                elif (t - 1, do + 14) in bb.out_store:
                    z2deps.append(bb.out_store[(t - 1, do + 14)])
                z2 = bb.op(zeng, (lambda e, d=ob[:, :], do=do, zt=zt: e.tensor_tensor(
                    d, zt[:, :] if zt is tmpf[2] else zt, rs_bc[:, :],
                    ALU.mult)), z2deps, zsem)
                bb.out_store[(t, do)] = bb.dma(
                    "G", out_own[do, :, 512 * t:512 * (t + 1)], ob[:, :], deps=[z2], inc="g")
            bb.last_out2 = [bb.out_store[(t, 14)], bb.out_store[(t, 15)]]
            bb.last_fin = z2
        ctx2.close()

    if do_p2:
        bb.w("G", ("g", bb.semcnt["g"]))
    else:
        bb.w("Y", ("y", bb.semcnt["y"]))

    with ExitStack() as rctx:
        sems = {n: rctx.enter_context(nc.semaphore(f"sem_{n}"))
                for n in ["T", "V", "S", "g", "y", "c", "P"]}
        block = rctx.enter_context(nc.Block())

        def mkreplay(key):
            def f(eng):
                for item in bb.prog[key]:
                    if item[0] == "w":
                        eng.wait_ge(sems[item[1]], item[2])
                    else:
                        ins = item[1](eng)
                        for (s, n) in item[2]:
                            ins.then_inc(sems[s], n)
            return f

        block.tensor(mkreplay("T"))
        block.vector(mkreplay("V"))
        block.scalar(mkreplay("S"))
        block.gpsimd(mkreplay("G"))
        block.sync(mkreplay("Y"))

    ctx_all.close()
    return nc


def host_constants():
    c_ident = np.eye(128, dtype=np.float32).astype(BFD)
    c_ones_col = np.ones((128, 1), np.float32).astype(BFD)
    c_ones_row = np.ones((1, 128), np.float32)
    masks = np.zeros((128, 4, 512), np.float32)
    kloc = np.arange(128)[:, None]
    qloc = np.arange(512)[None, :]
    for j in range(4):
        masks[:, j, :] = (qloc >= 128 * j + kloc).astype(np.float32)
    c_masks = masks.astype(BFD)
    inv_freq = 1.0 / (10000.0 ** (np.arange(0, HDK, 2, dtype=np.float64) / HDK))
    c_rope = np.zeros((128, 4, 128), np.float64)
    for l in range(4):
        cosv = np.cos(l * inv_freq)
        sinv = np.sin(l * inv_freq)
        Rl = np.zeros((64, 64))
        for i in range(32):
            Rl[i, i] = cosv[i]
            Rl[i, i + 32] = -sinv[i]
            Rl[i + 32, i + 32] = cosv[i]
            Rl[i + 32, i] = sinv[i]
        R2 = np.zeros((128, 128))
        R2[0:64, 0:64] = Rl
        R2[64:128, 64:128] = Rl
        c_rope[:, l, :] = R2.T
    c_rope = c_rope.astype(np.float32).astype(BFD)
    sel16 = np.zeros((16, 16, 128), np.float32)
    for cc in range(16):
        sel16[cc, cc, :] = 1.0
    sel128 = np.zeros((128, 16, 16), np.float32)
    for cc in range(16):
        sel128[:, cc, cc] = 1.0
    dotsel = np.zeros((128, 8, 16), np.float32)
    for c8 in range(8):
        dotsel[0:64, c8, 2 * c8] = 1.0
        dotsel[64:128, c8, 2 * c8 + 1] = 1.0
    return dict(c_ident=c_ident, c_ones_col=c_ones_col, c_ones_row=c_ones_row,
                c_masks=c_masks, c_rope=c_rope, c_sel16=sel16.astype(BFD),
                c_sel128=sel128.astype(BFD), c_dotsel=dotsel.astype(BFD))


def chunk_lhsT(M):
    din, dout = M.shape
    return np.ascontiguousarray(M.reshape(16, 128, dout // 128, 128).transpose(2, 1, 0, 3))


def host_phase1_inputs(x, wq, wk, wv):
    xT = np.ascontiguousarray(x.reshape(B, S, 16, 128).transpose(0, 3, 2, 1))
    percore = []
    for c in range(NCORE):
        wt = np.zeros((128, 3, LH, 16, 128), np.float32)
        for pi, W in enumerate([wq, wk, wv]):
            for lhh in range(LH):
                h = LH * c + lhh
                Wh = W[128 * h:128 * (h + 1), :]
                wt[:, pi, lhh, :, :] = Wh.T.reshape(16, 128, 128).transpose(1, 0, 2)
        percore.append(np.ascontiguousarray(wt))
    return xT, percore


def host_phase2_inputs(x, wo, bo, conv_w, Wq, Wk, Wv, Wg, router_w, Wout):
    shared = dict(
        w_woT=chunk_lhsT(np.ascontiguousarray(wo.T)),
        w_wq=chunk_lhsT(Wq), w_wk=chunk_lhsT(Wk), w_wv=chunk_lhsT(Wv),
        w_wg=chunk_lhsT(Wg), w_wout=chunk_lhsT(Wout))
    rwf = np.zeros((128, 2, 16, 16), np.float32)
    for cc in range(16):
        rwf[:, 0, cc, cc] = router_w[cc, :, 0]
        rwf[:, 1, cc, cc] = router_w[cc, :, 1]
    shared["rw_f"] = rwf
    shared["bo_f"] = np.ascontiguousarray(bo.reshape(16, 128).T.reshape(128, 16, 1))
    shared["cw0_f"] = np.ascontiguousarray(conv_w[:, 0].reshape(16, 128).T.reshape(128, 16, 1))
    shared["cw1_f"] = np.ascontiguousarray(conv_w[:, 1].reshape(16, 128).T.reshape(128, 16, 1))
    percore = []
    for c in range(NCORE):
        xo = x[:, SC * c:SC * (c + 1), :].reshape(B, NT, 128, 16, 128)
        percore.append(np.ascontiguousarray(
            xo.transpose(1, 4, 3, 0, 2).reshape(NT, 128, 16, 512)))
    return shared, percore


def assemble_output(results):
    out = np.empty((B, S, D), np.float32)
    for c in range(NCORE):
        oc = np.asarray(results[c]["out_own"], np.float32)
        oc = oc.reshape(16, 128, NT, B, 128).transpose(3, 2, 4, 0, 1)
        out[:, SC * c:SC * (c + 1), :] = oc.reshape(B, SC, D)
    return out


def kernel(**inputs):
    x = np.ascontiguousarray(np.asarray(inputs["L0_attn_x"], np.float32))
    consts = host_constants()
    xT, wqkvT_pc = host_phase1_inputs(
        x, np.asarray(inputs["L0_attn_wq"], np.float32),
        np.asarray(inputs["L0_attn_wk"], np.float32),
        np.asarray(inputs["L0_attn_wv"], np.float32))
    shared2, xown_pc = host_phase2_inputs(
        x, np.asarray(inputs["L0_attn_wo"], np.float32),
        np.asarray(inputs["L0_attn_bo"], np.float32),
        np.asarray(inputs["L0_ffn_conv_w"], np.float32),
        np.asarray(inputs["L0_ffn_Wq"], np.float32),
        np.asarray(inputs["L0_ffn_Wk"], np.float32),
        np.asarray(inputs["L0_ffn_Wv"], np.float32),
        np.asarray(inputs["L0_ffn_Wg"], np.float32),
        np.asarray(inputs["L0_ffn_router_w"], np.float32),
        np.asarray(inputs["L0_ffn_Wout"], np.float32))
    in_maps = []
    for c in range(NCORE):
        m = dict(consts)
        m.update(shared2)
        m["xT"] = xT
        m["wqkvT"] = wqkvT_pc[c]
        m["x_own"] = xown_pc[c]
        in_maps.append(m)
    nc = build_program(do_p1=True, do_p2=True)
    res = run_bass_kernel_spmd(nc, in_maps, core_ids=list(range(NCORE)))
    return assemble_output(res.results)


# version 23
# speedup vs baseline: 1.0258x; 1.0258x over previous
import numpy as np
import ml_dtypes
from contextlib import ExitStack
from collections import defaultdict
from math import sqrt

import concourse.bass as bass
import concourse.mybir as mybir
from concourse.bass_utils import run_bass_kernel_spmd

F32 = mybir.dt.float32
BF16 = mybir.dt.bfloat16
AF = mybir.ActivationFunctionType
ALU = mybir.AluOpType
BFD = ml_dtypes.bfloat16

B, S, D, H, HD = 4, 4096, 2048, 16, 128
NCORE = 8
LH = 2
SC = S // NCORE
NT = 4
DK, DV, HDK, HDV = 1024, 2048, 64, 128
USHIFT = 12.0
SCL = 1.0 / sqrt(HD)
EPS = 1e-5
NQG = S // 512


class BB:
    ENGS = ("T", "V", "S", "G", "Y")

    def __init__(self):
        self.prog = {e: [] for e in self.ENGS}
        self.semcnt = defaultdict(int)
        self.maxwait = defaultdict(int)
        self.bank_w = {}
        self.bank_r = {}

    def w(self, eng, tok):
        if tok is None:
            return
        sem, val = tok
        if val <= 0 or self.maxwait[(eng, sem)] >= val:
            return
        self.maxwait[(eng, sem)] = val
        self.prog[eng].append(("w", sem, val))

    def op(self, eng, fn, deps=(), inc=None, inc_n=1):
        for d in deps:
            self.w(eng, d)
        incs = []
        tok = None
        if inc is not None:
            self.semcnt[inc] += inc_n
            tok = (inc, self.semcnt[inc])
            incs.append((inc, inc_n))
        self.prog[eng].append(("op", fn, incs))
        return tok

    def dma(self, eng, out, in_, deps=(), inc=None):
        return self.op(eng, (lambda e, o=out, i=in_: e.dma_start(out=o, in_=i)),
                       deps, inc, inc_n=16)

    def bw_deps(self, banks):
        return [self.bank_r[b] for b in banks if self.bank_r.get(b)]

    def br_deps(self, banks):
        return [self.bank_w[b] for b in banks if self.bank_w.get(b)]

    def set_w(self, banks, tok):
        for b in banks:
            self.bank_w[b] = tok

    def set_r(self, banks, tok):
        for b in banks:
            self.bank_r[b] = tok


def mm_group(bb, banks, pairs, deps, inc="T"):
    n = len(pairs)
    alldeps = list(deps) + bb.bw_deps(banks)
    tok = None
    for i, (o, l, r) in enumerate(pairs):
        st, sp = (i == 0), (i == n - 1)
        tok = bb.op("T", (lambda e, o=o, l=l, r=r, st=st, sp=sp:
                          e.matmul(o, l, r, start=st, stop=sp)),
                    alldeps if i == 0 else (), inc if sp else None)
    bb.set_w(banks, tok)
    return tok


def build_program(do_p1=True, do_p2=True, dbg_ao=False, dbg_p2=False):
    nc = bass.Bass("TRN2", target_bir_lowering=False)

    if do_p1:
        xT = nc.dram_tensor("xT", [B, 128, 16, S], F32, kind="ExternalInput")
        wqkvT = nc.dram_tensor("wqkvT", [128, 3, LH, 16, 128], F32, kind="ExternalInput")
    if do_p2:
        w_woT = nc.dram_tensor("w_woT", [16, 128, 16, 128], F32, kind="ExternalInput")
        w_wq = nc.dram_tensor("w_wq", [8, 128, 16, 128], F32, kind="ExternalInput")
        w_wk = nc.dram_tensor("w_wk", [8, 128, 16, 128], F32, kind="ExternalInput")
        w_wv = nc.dram_tensor("w_wv", [16, 128, 16, 128], F32, kind="ExternalInput")
        w_wg = nc.dram_tensor("w_wg", [16, 128, 16, 128], F32, kind="ExternalInput")
        w_wout = nc.dram_tensor("w_wout", [16, 128, 16, 128], F32, kind="ExternalInput")
        rw_f = nc.dram_tensor("rw_f", [128, 2, 16, 16], F32, kind="ExternalInput")
        bo_f = nc.dram_tensor("bo_f", [128, 16, 1], F32, kind="ExternalInput")
        cw0_f = nc.dram_tensor("cw0_f", [128, 16, 1], F32, kind="ExternalInput")
        cw1_f = nc.dram_tensor("cw1_f", [128, 16, 1], F32, kind="ExternalInput")
        x_own = nc.dram_tensor("x_own", [NT, 128, 16, 512], F32, kind="ExternalInput")
        out_own = nc.dram_tensor("out_own", [16, 128, 2048], F32, kind="ExternalOutput")
        if dbg_p2:
            dbg = {}
            for nm, shp, dt in [("dbg_u", [128, 16, 512], BF16), ("dbg_xc", [128, 16, 512], BF16),
                                ("dbg_qrot", [128, 8, 512], BF16), ("dbg_krot", [128, 8, 512], BF16),
                                ("dbg_v", [128, 16, 512], BF16), ("dbg_g", [128, 16, 512], BF16),
                                ("dbg_rows", [16, 10, 512], F32), ("dbg_o", [128, 16, 512], F32),
                                ("dbg_ao2", [128, 16, 512], BF16),
                                ("dbg_ao4", [NT, 128, 16, 512], BF16),
                                ("dbg_u4", [NT, 128, 16, 512], BF16)]:
                dbg[nm] = nc.dram_tensor(nm, shp, dt, kind="ExternalOutput")
    c_ident = nc.dram_tensor("c_ident", [128, 128], BF16, kind="ExternalInput")
    c_ones_col = nc.dram_tensor("c_ones_col", [128, 1], BF16, kind="ExternalInput")
    c_ones_row = nc.dram_tensor("c_ones_row", [1, 128], F32, kind="ExternalInput")
    c_masks = nc.dram_tensor("c_masks", [128, 4, 512], BF16, kind="ExternalInput")
    c_rope = nc.dram_tensor("c_rope", [128, 4, 128], BF16, kind="ExternalInput")
    c_sel16 = nc.dram_tensor("c_sel16", [16, 16, 128], BF16, kind="ExternalInput")
    c_sel128 = nc.dram_tensor("c_sel128", [128, 16, 16], BF16, kind="ExternalInput")
    c_dotsel = nc.dram_tensor("c_dotsel", [128, 8, 16], BF16, kind="ExternalInput")

    full = do_p1 and do_p2
    ao_kind = "ExternalOutput" if (dbg_ao and not full) else "Internal"
    if do_p1:
        ao = nc.dram_tensor("ao", [B * LH * 128, S], BF16, kind=ao_kind)
    if do_p2:
        ag_kind = "Internal" if full else "ExternalInput"
        ag_space = "Shared" if full else "Local"
        ag = nc.dram_tensor("ag", [NCORE * B * LH * 128, S], BF16, kind=ag_kind,
                            addr_space=ag_space)
        aog = nc.dram_tensor("aog", [NCORE * B * LH * 128, 128], BF16, kind="Internal")
        cwoT = nc.dram_tensor("cwoT", [16, 128, 16, 128], BF16, kind="Internal")
        cwq = nc.dram_tensor("cwq", [8, 128, 16, 128], BF16, kind="Internal")
        cwk = nc.dram_tensor("cwk", [8, 128, 16, 128], BF16, kind="Internal")
        cwv = nc.dram_tensor("cwv", [16, 128, 16, 128], BF16, kind="Internal")
        cwg = nc.dram_tensor("cwg", [16, 128, 16, 128], BF16, kind="Internal")
        cwout = nc.dram_tensor("cwout", [16, 128, 16, 128], BF16, kind="Internal")

    bb = BB()
    ctx_all = ExitStack()
    E = ctx_all.enter_context

    id_sb = E(nc.sbuf_tensor("id_sb", [128, 128], BF16))
    onec_sb = E(nc.sbuf_tensor("onec_sb", [128, 1], BF16))
    oner_sb = E(nc.sbuf_tensor("oner_sb", [1, 128], F32))
    rope_sb = E(nc.sbuf_tensor("rope_sb", [128, 4, 128], BF16))
    sel16_sb = E(nc.sbuf_tensor("sel16_sb", [16, 16, 128], BF16))
    sel128_sb = E(nc.sbuf_tensor("sel128_sb", [128, 16, 16], BF16))
    dotsel_sb = E(nc.sbuf_tensor("dotsel_sb", [128, 8, 16], BF16))
    if do_p2:
        rw_sb = E(nc.sbuf_tensor("rw_sb", [128, 2, 16, 16], BF16))
        bo_sb = E(nc.sbuf_tensor("bo_sb", [128, 16, 1], F32))
        cw0_sb = E(nc.sbuf_tensor("cw0_sb", [128, 16, 1], F32))
        cw1_sb = E(nc.sbuf_tensor("cw1_sb", [128, 16, 1], F32))
    cpool = E(nc.sbuf_tensor("cpool", [128, 4], F32))
    tok_cp = None
    for col, val in [(0, -USHIFT), (1, SCL), (2, EPS), (3, -0.5)]:
        tok_cp = bb.op("V", (lambda e, col=col, val=val: e.memset(cpool[:, col:col + 1], val)),
                       (), inc="V")
    ap_ushift = cpool[:, 0:1]
    ap_scl = cpool[:, 1:2]
    ap_eps1 = cpool[0:1, 2:3]
    ap_eps16 = cpool[0:16, 2:3]
    ap_mhalf1 = cpool[0:1, 3:4]
    ap_mhalf16 = cpool[0:16, 3:4]
    if do_p2:
        wbuf = [E(nc.sbuf_tensor(f"wbuf{i}", [128, 16, 128], BF16)) for i in range(4)]
        xch = [E(nc.sbuf_tensor(f"xch{i}", [128, 512], F32)) for i in range(2)]

    psS0 = E(nc.psum_tensor("psS0", [128, 1024], F32))
    psS1 = E(nc.psum_tensor("psS1", [128, 1024], F32))
    psO = E(nc.psum_tensor("psO", [128, 512], F32))
    psD = E(nc.psum_tensor("psD", [1, 512], F32))
    psP0 = E(nc.psum_tensor("psP0", [128, 512], F32))
    psP1 = E(nc.psum_tensor("psP1", [128, 512], F32))

    tok_c = None
    for dst, src in [(id_sb[:, :], c_ident[:, :]), (onec_sb[:, :], c_ones_col[:, :]),
                     (oner_sb[:, :], c_ones_row[:, :]), (rope_sb[:, :, :], c_rope[:, :, :]),
                     (sel16_sb[:, :, :], c_sel16[:, :, :]), (sel128_sb[:, :, :], c_sel128[:, :, :]),
                     (dotsel_sb[:, :, :], c_dotsel[:, :, :])]:
        tok_c = bb.dma("Y", dst, src, inc="y")
    tok_rw = tok_c
    if do_p2:
        for dst, src in [(bo_sb[:, :, :], bo_f[:, :, :]), (cw0_sb[:, :, :], cw0_f[:, :, :]),
                         (cw1_sb[:, :, :], cw1_f[:, :, :])]:
            tok_c = bb.dma("Y", dst, src, inc="y")
        tok_rw = bb.dma("G", rw_sb[:, :, :, :], rw_f[:, :, :, :], inc="g")

    cache_tok = [None]
    cache_jobs = []
    cache_i = [0]
    if do_p2:
        for dstt, srct, ndo in [(cwoT, w_woT, 16), (cwq, w_wq, 8), (cwk, w_wk, 8),
                                (cwv, w_wv, 16), (cwg, w_wg, 16), (cwout, w_wout, 16)]:
            for do in range(ndo):
                cache_jobs.append((dstt[do, :, :, :], srct[do, :, :, :]))

    def drain_cache(n):
        for _ in range(n):
            if cache_i[0] >= len(cache_jobs):
                return
            dst, src = cache_jobs[cache_i[0]]
            cache_tok[0] = bb.dma("G", dst, src, inc="g")
            cache_i[0] += 1

    if do_p1:
        ctx1 = ExitStack()
        E1 = ctx1.enter_context
        wT_sb = E1(nc.sbuf_tensor("wT_sb", [128, 3, LH, 16, 128], BF16))
        xbuf = [E1(nc.sbuf_tensor(f"xbuf{i}", [128, 16, 512], BF16)) for i in range(2)]
        q_sb = [E1(nc.sbuf_tensor(f"q_sb{i}", [128, 8, 512], BF16)) for i in range(LH)]
        k_sb = [E1(nc.sbuf_tensor(f"k_sb{i}", [128, 8, 512], BF16)) for i in range(LH)]
        v_tok = [E1(nc.sbuf_tensor(f"v_tok{i}", [128, 32, 128], BF16)) for i in range(LH)]
        vtmp = [E1(nc.sbuf_tensor(f"vtmp{i}", [128, 512], BF16)) for i in range(LH)]
        masks_sb = E1(nc.sbuf_tensor("masks_sb", [128, 4, 512], BF16))
        pbuf = [E1(nc.sbuf_tensor(f"pbuf{i}", [128, 1024], BF16)) for i in range(2)]
        drow = E1(nc.sbuf_tensor("drow", [1, 512], F32))
        drec = E1(nc.sbuf_tensor("drec", [1, 512], F32))
        bcn = E1(nc.sbuf_tensor("bcn", [128, 512], F32))
        aobuf = [E1(nc.sbuf_tensor(f"aobuf{i}", [128, 512], BF16)) for i in range(2)]

        tok_masks = bb.dma("Y", masks_sb[:, :, :], c_masks[:, :, :], inc="y")
        tok_wT = bb.dma("G", wT_sb[:, :, :, :, :], wqkvT[:, :, :, :, :], inc="g")

        PB = {"P0": (psP0, ("p0",)), "P1": (psP1, ("p1",)), "O": (psO, ("o",))}
        pcyc = ["P0", "P1", "O"]

        tok_xdma = {}
        tok_xfree = {}
        ao_store = {}
        for b in range(B):
            evV = evS = None
            for tg in range(NQG):
                par = tg % 2
                fd = []
                prevkey = (b, tg - 2) if tg >= 2 else (b - 1, tg + 6)
                if prevkey in tok_xfree:
                    fd.append(tok_xfree[prevkey])
                for qq in range(4):
                    tok_xdma[(b, tg)] = bb.dma(
                        "G", xbuf[par][:, 4 * qq:4 * qq + 4, :],
                        xT[b, :, 4 * qq:4 * qq + 4, 512 * tg:512 * (tg + 1)],
                        deps=(fd + [tok_wT]) if qq == 0 else (), inc="g")
                outs6 = [(0, 0), (0, 1), (1, 0), (1, 1), (2, 0), (2, 1)]
                for oi6, (proj, lh) in enumerate(outs6):
                    psum, bank = PB[pcyc[oi6 % 3]]
                    pairs = [(psum[:, :], wT_sb[:, proj, lh, cc, :], xbuf[par][:, cc, :])
                             for cc in range(16)]
                    tokw = mm_group(bb, bank, pairs, [tok_xdma[(b, tg)]])
                    if oi6 == 5:
                        tok_xfree[(b, tg)] = tokw
                    if proj == 0:
                        dst = q_sb[lh][:, tg, :]
                    elif proj == 1:
                        dst = k_sb[lh][:, tg, :]
                    else:
                        dst = vtmp[lh][:, :]
                    if oi6 % 2 == 0:
                        tok = bb.op("V", (lambda e, d=dst, s=psum: e.tensor_copy(d, s[:, :])),
                                    [tokw], inc="V")
                        evV = tok
                    else:
                        tok = bb.op("S", (lambda e, d=dst, s=psum:
                                          e.activation(d, s[:, :], AF.Identity)),
                                    [tokw], inc="S")
                        evS = tok
                    bb.set_r(bank, tok)
                    if proj == 2:
                        fpsum, fbank = PB[pcyc[lh % 3]]
                        fdeps = [tok] + bb.bw_deps(fbank)
                        ftok = None
                        for jj in range(4):
                            ftok = bb.op("T", (lambda e, o=fpsum, src=vtmp[lh], jj=jj:
                                               e.matmul(o[:, 128 * jj:128 * (jj + 1)],
                                                        src[:, 128 * jj:128 * (jj + 1)],
                                                        id_sb[:, :], start=True, stop=True)),
                                         fdeps if jj == 0 else (), "T" if jj == 3 else None)
                        bb.set_w(fbank, ftok)
                        etok = bb.op("V", (lambda e, d=v_tok[lh][:, 4 * tg:4 * tg + 4, :], s=fpsum:
                                           e.tensor_copy(d, s[:, :].rearrange("p (a c) -> p a c", a=4))),
                                    [ftok], inc="V")
                        bb.set_r(fbank, etok)
                        evV = etok
                drain_cache(2)
            stageA_deps = [evV, evS, tok_masks, tok_c, tok_cp]

            for lh in range(LH):
                pairs = [(qg, kp2) for qg in range(NQG) for kp2 in range(2 * qg + 2)]
                stoks = {}
                cons = {}
                qg_ctok = {}
                pending = [None]

                def emit_S(gp, extra=()):
                    qg, kp2 = pairs[gp]
                    kb0, kb1 = 2 * kp2, 2 * kp2 + 1
                    Sps = psS0 if gp % 2 == 0 else psS1
                    sbanks = ("s0a", "s0b") if gp % 2 == 0 else ("s1a", "s1b")
                    deps = list(extra) + bb.bw_deps(sbanks)
                    tok = None
                    for half, kb in ((0, kb0), (1, kb1)):
                        tok = bb.op(
                            "T", (lambda e, o=Sps[:, 512 * half:512 * (half + 1)],
                                  l=k_sb[lh][:, kb // 4, 128 * (kb % 4):128 * (kb % 4) + 128],
                                  r=q_sb[lh][:, qg, :]:
                                  e.matmul(o, l, r, start=True, stop=True)),
                            deps if half == 0 else (), "T" if half == 1 else None)
                    bb.set_w(sbanks, tok)
                    stoks[gp] = tok

                def emit_norm(qg, ctok):
                    obank = "o" if qg % 2 == 0 else "p1"
                    psO_t = psO if qg % 2 == 0 else psP1
                    t2 = bb.op("V", (lambda e: e.reciprocal(drec[:, :], drow[:, :])),
                               [bb.bank_r["d"]], inc="V")
                    t3 = bb.op("T", (lambda e: e.matmul(psP0[:, :], oner_sb[:, :], drec[:, :],
                                                        start=True, stop=True)),
                               [t2] + bb.bw_deps(("p0",)), inc="T")
                    bb.set_w(("p0",), t3)
                    t4 = bb.op("V", (lambda e: e.tensor_copy(bcn[:, :], psP0[:, :])), [t3], inc="V")
                    bb.set_r(("p0",), t4)
                    abuf = aobuf[qg % 2]
                    t5deps = [t4, ctok]
                    if (b, lh, qg - 2) in ao_store:
                        t5deps.append(ao_store[(b, lh, qg - 2)])
                    elif (b, lh - 1, qg + 6) in ao_store:
                        t5deps.append(ao_store[(b, lh - 1, qg + 6)])
                    t5 = bb.op("V", (lambda e, d=abuf[:, :], s=psO_t:
                                     e.tensor_tensor(d, s[:, :], bcn[:, :], ALU.mult)),
                               t5deps, inc="V")
                    bb.set_r((obank,), t5)
                    ao_store[(b, lh, qg)] = bb.dma(
                        "Y", ao[256 * b + 128 * lh:256 * b + 128 * (lh + 1),
                                512 * qg:512 * (qg + 1)],
                        abuf[:, :], deps=[t5], inc="y")

                emit_S(0, stageA_deps)
                for gp, (qg, kp2) in enumerate(pairs):
                    npairs = 2 * qg + 2
                    kb0, kb1 = 2 * kp2, 2 * kp2 + 1
                    if gp + 1 < len(pairs):
                        emit_S(gp + 1)
                    Sps = psS0 if gp % 2 == 0 else psS1
                    sbanks = ("s0a", "s0b") if gp % 2 == 0 else ("s1a", "s1b")
                    pb = pbuf[gp % 2]
                    edeps = [stoks[gp]]
                    if (gp - 2) in cons:
                        edeps.append(cons[gp - 2])
                    etok = bb.op("S", (lambda e, d=pb[:, :], s=Sps[:, :]:
                                       e.activation(d, s, AF.Exp, bias=ap_ushift,
                                                    scale=ap_scl)),
                                 edeps, inc="S")
                    bb.set_r(sbanks, etok)
                    mtoks = []
                    for half, kb in ((0, kb0), (1, kb1)):
                        if kb >= 4 * qg:
                            j = kb - 4 * qg
                            w = 128 * (j + 1)
                            mtoks.append(bb.op(
                                "V", (lambda e, d=pb[:, 512 * half:512 * half + w],
                                      m=masks_sb[:, j, 0:w]:
                                      e.tensor_tensor(d, d, m, ALU.mult)),
                                [etok], inc="V"))
                    obank = "o" if qg % 2 == 0 else "p1"
                    psO_t = psO if qg % 2 == 0 else psP1
                    avdeps = [etok] + mtoks
                    if kp2 == 0:
                        avdeps += bb.bw_deps((obank, "d"))
                    last = (kp2 == npairs - 1)
                    bb.op("T", (lambda e, l=v_tok[lh][:, kb0, :], r=pb[:, 0:512], o=psO_t,
                                st=(kp2 == 0): e.matmul(o[:, :], l, r, start=st, stop=False)),
                          avdeps)
                    bb.op("T", (lambda e, l=v_tok[lh][:, kb1, :], r=pb[:, 512:1024], o=psO_t,
                                sp=last: e.matmul(o[:, :], l, r, start=False, stop=sp)), ())
                    bb.op("T", (lambda e, r=pb[:, 0:512], st=(kp2 == 0):
                                e.matmul(psD[:, :], onec_sb[:, :], r, start=st, stop=False)), ())
                    ctok = bb.op("T", (lambda e, r=pb[:, 512:1024], sp=last:
                                       e.matmul(psD[:, :], onec_sb[:, :], r, start=False, stop=sp)),
                                 (), "T")
                    cons[gp] = ctok
                    if pending[0] is not None:
                        pending[0]()
                        pending[0] = None
                    if last:
                        bb.set_w((obank, "d"), ctok)
                        qg_ctok[qg] = ctok
                        t1 = bb.op("V", (lambda e: e.tensor_copy(drow[:, :], psD[:, :])),
                                   [ctok], inc="V")
                        bb.set_r(("d",), t1)
                        pending[0] = (lambda qg=qg, ctok=ctok: emit_norm(qg, ctok))
                if pending[0] is not None:
                    pending[0]()
                    pending[0] = None
                drain_cache(4)


        drain_cache(len(cache_jobs))
        ctx1.close()

    if full:
        cc_deps = [cache_tok[0], tok_rw, ("y", bb.semcnt["y"])]
        tok_cc = bb.op("G", (lambda e: e.collective_compute(
            "AllGather", ALU.bypass, replica_groups=[list(range(NCORE))],
            ins=[ao.ap().opt()], outs=[ag.ap().opt()])),
            [d for d in cc_deps if d], inc="c")
    elif do_p1:
        tok_cc = None
    else:
        drain_cache(len(cache_jobs))
        tok_cc = None

    if do_p2:
        ctx2 = ExitStack()
        E2 = ctx2.enter_context
        aov_sb = E2(nc.sbuf_tensor("aov_sb", [128, 16, 512], BF16))
        v2only = E2(nc.sbuf_tensor("v2only", [128, 16, 512], BF16))
        u_sb = E2(nc.sbuf_tensor("u_sb", [128, 16, 512], BF16))
        xc_sb = E2(nc.sbuf_tensor("xc_sb", [128, 16, 512], BF16))
        qrot = E2(nc.sbuf_tensor("qrot", [128, 8, 512], BF16))
        krot = E2(nc.sbuf_tensor("krot", [128, 8, 512], BF16))
        g_sb = E2(nc.sbuf_tensor("g_sb", [128, 16, 512], BF16))
        big4 = E2(nc.sbuf_tensor("big4", [128, 16, 512], F32))
        tmpf = [E2(nc.sbuf_tensor(f"tmpf{i}", [128, 512], F32)) for i in range(3)]
        tmpf.append(tmpf[0])
        smb = [E2(nc.sbuf_tensor(f"smb{i}", [128, 512], BF16)) for i in range(4)]
        mu_bc = E2(nc.sbuf_tensor("mu_bc", [128, 512], F32))
        rs_bc = E2(nc.sbuf_tensor("rs_bc", [128, 512], F32))
        outb = [tmpf[0], tmpf[1]]
        prodb = [E2(nc.sbuf_tensor(f"prodb{i}", [128, 128], BF16)) for i in range(2)]
        qkw_sb = E2(nc.sbuf_tensor("qkw_sb", [16, 128], BF16))
        wrow = E2(nc.sbuf_tensor("wrow", [16, 128], F32))
        wrow2 = E2(nc.sbuf_tensor("wrow2", [16, 128], F32))
        rn = ["l0r", "l1r", "s0r", "i0r", "i1r", "ms0r", "ms1r",
              "c0r", "c1r", "t0r", "t2r"]
        R = {n: E2(nc.sbuf_tensor(n, [16, 512], F32)) for n in rn}
        R["si0r"] = R["l0r"]
        R["si1r"] = R["l1r"]
        R["t1r"] = R["s0r"]
        murb = E2(nc.sbuf_tensor("murb", [16, 512], BF16))
        rsrb = E2(nc.sbuf_tensor("rsrb", [16, 512], BF16))
        srow1 = E2(nc.sbuf_tensor("srow1", [1, 512], F32))
        qrow1 = E2(nc.sbuf_tensor("qrow1", [1, 512], F32))
        murow = E2(nc.sbuf_tensor("murow", [1, 512], F32))
        varrow = qrow1
        rstdrow = qrow1
        mu2row = srow1
        hrow = {n: E2(nc.sbuf_tensor("h" + n, [16, 512], F32))
                for n in ["sum", "var", "rstd"]}
        hrow["mu"] = hrow["sum"]

        pPA, pPB = psP0, psP1
        pS1, pS2 = psD, psO[0:1, :]
        pR0, pR1 = psS1[0:16, 0:512], psS1[0:16, 512:1024]
        pDOT = psS1[0:16, 0:128]
        pBC1, pBC2 = psS0[:, 0:512], psS0[:, 512:1024]
        BK = {"PA": ("p0",), "PB": ("p1",), "S1": ("d",), "S2": ("o",),
              "R0": ("s1a",), "R1": ("s1b",), "BC1": ("s0a",), "BC2": ("s0b",)}

        pid = nc.partition_id()
        wcfg = {"wo": (cwoT, 16), "wq": (cwq, 8), "wk": (cwk, 8),
                "wv": (cwv, 16), "wg": (cwg, 16), "wout": (cwout, 16)}
        wbuf_free = {}
        wload_i = [0]

        def VOP(fn, deps):
            return bb.op("V", fn, deps, "V")

        def SOP(fn, deps):
            return bb.op("S", fn, deps, "S")

        def wload(name, do, gate):
            i = wload_i[0]
            dstt = wbuf[i % 4]
            t, _ = wcfg[name]
            dd = list(gate)
            if (i - 4) in wbuf_free:
                dd.append(wbuf_free[i - 4])
            tok = bb.dma("Y", dstt[:, :, :], t[do, :, :, :], deps=dd, inc="y")
            wload_i[0] += 1
            return i, dstt, tok

        def ln_rows(tok_stats, ndim):
            r1 = VOP((lambda e: e.tensor_copy(srow1[:, :], pS1[:, :])), [tok_stats])
            bb.set_r(BK["S1"], r1)
            r2 = VOP((lambda e: e.tensor_copy(qrow1[:, :], pS2[:, :])), [tok_stats])
            bb.set_r(BK["S2"], r2)
            r3 = VOP((lambda e: e.tensor_scalar_mul(murow[:, :], srow1[:, :], 1.0 / ndim)), [r1])
            r4 = SOP((lambda e: e.activation(mu2row[:, :], murow[:, :], AF.Square)), [r3])
            r5 = VOP((lambda e: e.tensor_scalar_mul(varrow[:, :], qrow1[:, :], 1.0 / ndim)), [r2])
            r6 = VOP((lambda e: e.tensor_tensor(varrow[:, :], varrow[:, :], mu2row[:, :],
                                                ALU.subtract)), [r5, r4])
            r7 = SOP((lambda e: e.activation(varrow[:, :], varrow[:, :], AF.Ln,
                                             bias=ap_eps1)), [r6, tok_cp])
            r8 = SOP((lambda e: e.activation(rstdrow[:, :], varrow[:, :], AF.Exp,
                                             scale=ap_mhalf1)), [r7])
            t1 = bb.op("T", (lambda e: e.matmul(pBC1[:, :], oner_sb[:, :], murow[:, :],
                                                start=True, stop=True)),
                       [r3] + bb.bw_deps(BK["BC1"]), "T")
            bb.set_w(BK["BC1"], t1)
            t2 = bb.op("T", (lambda e: e.matmul(pBC2[:, :], oner_sb[:, :], rstdrow[:, :],
                                                start=True, stop=True)),
                       [r8] + bb.bw_deps(BK["BC2"]), "T")
            bb.set_w(BK["BC2"], t2)
            c1 = VOP((lambda e: e.tensor_copy(mu_bc[:, :], pBC1[:, :])), [t1])
            bb.set_r(BK["BC1"], c1)
            c2 = VOP((lambda e: e.tensor_copy(rs_bc[:, :], pBC2[:, :])), [t2])
            bb.set_r(BK["BC2"], c2)
            return c1, c2

        cache_gate = [cache_tok[0], tok_rw] if cache_tok[0] else []
        for t in range(NT):
            gate = ([tok_cc] if (tok_cc and t == 0) else [])
            if t == 0:
                gate += cache_gate
            gate = [g for g in gate if g]
            wgate = cache_gate if t == 0 else []



            def emit_ao_prefetch(tt, extra):
                def mk_bulk(e, tt=tt):
                    off = e.snap(pid * SC + 128 * tt, min_val=0, max_val=S - 128)
                    return e.dma_start(out=aog[:, :], in_=ag[:, bass.ds(off, 128)])
                btok = bb.op("G", mk_bulk, extra, inc="g", inc_n=16)
                atok = None
                aogv = aog.rearrange("(r b f) s -> f r b s", r=NCORE, b=B)
                for cc in range(16):
                    atok = bb.dma("G", aov_sb[:, cc, :].rearrange("p (a c) -> p a c", a=4),
                                  aogv[128 * (cc % 2):128 * (cc % 2) + 128, cc // 2, :, :],
                                  deps=[btok] if cc == 0 else (), inc="g")
                return atok

            if not hasattr(bb, "ao_pref"):
                bb.ao_pref = {}
            if t in bb.ao_pref:
                ao_tok = bb.ao_pref[t]
            else:
                ao_tok = emit_ao_prefetch(t, gate)

            if dbg_p2 and t == 0:
                bb.dma("Y", dbg["dbg_ao2"][:, :, :], aov_sb[:, :, :], deps=[ao_tok], inc="y")
            if dbg_p2:
                bb.dma("G", dbg["dbg_ao4"][t, :, :, :], aov_sb[:, :, :], deps=[ao_tok], inc="g")
            prev_evac = {}
            stat_toks = {}
            jobs = {}

            def emit_statsA(do, first, last):
                e2, e3, sq = jobs[do]
                d1 = [e2] + (bb.bw_deps(BK["S1"]) if first else [])
                bb.op("T", (lambda e, do=do, st=first, sp=last:
                            e.matmul(pS1[:, :], onec_sb[:, :], u_sb[:, do, :], start=st, stop=sp)),
                      d1, "T" if last else None)
                d2 = [e3] + (bb.bw_deps(BK["S2"]) if first else [])
                tk = bb.op("T", (lambda e, sq=sq, st=first, sp=last:
                                 e.matmul(pS2[:, :], onec_sb[:, :], sq[:, :], start=st, stop=sp)),
                           d2, "T")
                stat_toks[do] = tk
                return tk

            for do in range(16):
                wi, wt, wtok = wload("wo", do, wgate)
                xdeps = [prev_evac[do - 2]] if (do - 2) in prev_evac else \
                    list(getattr(bb, "last_xch", []))
                xtok = bb.dma("Y", xch[do % 2][:, :], x_own[t, :, do, :],
                              deps=xdeps, inc="y")
                ps = pPA if do % 2 == 0 else pPB
                bank = BK["PA"] if do % 2 == 0 else BK["PB"]
                pairs = [(ps[:, :], wt[:, cc, :], aov_sb[:, cc, :]) for cc in range(16)]
                mtok = mm_group(bb, bank, pairs, [wtok, ao_tok] + gate)
                wbuf_free[wi] = mtok
                if do == 15:
                    bb.wo_done = mtok
                    if t + 1 < NT:
                        bb.ao_pref[t + 1] = emit_ao_prefetch(t + 1, [mtok])
                tf = tmpf[do % 2]
                e1deps = [mtok]
                if (do - 2) in prev_evac:
                    e1deps.append(prev_evac[do - 2])
                elif hasattr(bb, "last_out2"):
                    e1deps.append(bb.last_out2[do % 2])
                e1 = SOP((lambda e, d=tf[:, :], s=ps, bias=bo_sb[:, do, :]:
                          e.activation(d, s[:, :], AF.Identity, bias=bias)), e1deps)
                bb.set_r(bank, e1)
                e2 = VOP((lambda e, do=do, a=tf[:, :], x=xch[do % 2][:, :]:
                          e.tensor_tensor(u_sb[:, do, :], a, x, ALU.add)), [e1, xtok])
                prev_evac[do] = e2
                sq = smb[do % 2]
                e3deps = [e2]
                if (do - 2) in stat_toks:
                    e3deps.append(stat_toks[do - 2])
                e3 = SOP((lambda e, d=sq[:, :], do=do: e.activation(d, u_sb[:, do, :], AF.Square)),
                         e3deps)
                jobs[do] = (e2, e3, sq)
                if do >= 1:
                    emit_statsA(do - 1, do - 1 == 0, False)
            tok_stats = emit_statsA(15, False, True)
            bb.set_w(BK["S1"], tok_stats)
            bb.set_w(BK["S2"], tok_stats)
            bb.last_xch = [prev_evac[14], prev_evac[15]]

            c1, c2 = ln_rows(tok_stats, 2048.0)
            u_toks = {}
            lf = getattr(bb, "last_fin", None)
            for do in range(16):
                eng = "V"
                sem = "V"
                a1 = bb.op(eng, (lambda e, do=do: e.tensor_tensor(
                    big4[:, do, :], u_sb[:, do, :], mu_bc[:, :], ALU.subtract)),
                    [c1, c2] + ([lf] if (lf and eng == "G") else []), sem)
                u_toks[do] = bb.op(eng, (lambda e, do=do: e.tensor_tensor(
                    u_sb[:, do, :], big4[:, do, :], rs_bc[:, :], ALU.mult)), [a1], sem)

            if dbg_p2 and t == 0:
                bb.dma("Y", dbg["dbg_u"][:, :, :], u_sb[:, :, :],
                       deps=list(u_toks.values()), inc="y")
            if dbg_p2:
                bb.dma("Y", dbg["dbg_u4"][t, :, :, :], u_sb[:, :, :],
                       deps=list(u_toks.values()), inc="y")
            xc_toks = {}
            for do in range(16):
                eng = "V"
                sem = "V"
                m1 = bb.op(eng, (lambda e, do=do: e.tensor_scalar_mul(
                    big4[:, do, 0:128], u_sb[:, do, 0:128], cw1_sb[:, do, :])),
                    [u_toks[do]], sem)
                m2 = bb.op(eng, (lambda e, do=do: e.tensor_scalar_mul(
                    big4[:, do, 128:512], u_sb[:, do, 128:512], cw1_sb[:, do, :])),
                    [m1], sem)
                m3 = bb.op(eng, (lambda e, do=do: e.scalar_tensor_tensor(
                    big4[:, do, 128:512], u_sb[:, do, 0:384], cw0_sb[:, do, :],
                    big4[:, do, 128:512], ALU.mult, ALU.add)), [m2], sem)
                xc_toks[do] = SOP((lambda e, do=do: e.activation(xc_sb[:, do, :],
                                                                big4[:, do, :], AF.Silu)), [m3])
            xc_all = list(xc_toks.values())

            if dbg_p2 and t == 0:
                bb.dma("Y", dbg["dbg_xc"][:, :, :], xc_sb[:, :, :], deps=xc_all, inc="y")
            def proj(name, ndo, rhs_sb, evac_cb, extra_first_deps=()):
                toks = []
                for do in range(ndo):
                    wi, wt, wtok = wload(name, do, wgate)
                    ps = pPA if do % 2 == 0 else pPB
                    bank = BK["PA"] if do % 2 == 0 else BK["PB"]
                    deps = [wtok] + (list(extra_first_deps) if do == 0 else [])
                    pairs = [(ps[:, :], wt[:, cc, :], rhs_sb[:, cc, :]) for cc in range(16)]
                    mtok = mm_group(bb, bank, pairs, deps)
                    wbuf_free[wi] = mtok
                    toks.append(evac_cb(do, mtok, ps, bank))
                return toks

            def rope_evac(dst_rot, act):
                def cb(do, mtok, ps, bank):
                    pre = smb[2 + do % 2]
                    e1deps = [mtok]
                    if hasattr(cb, "prev2") and (do - 2) in cb.prev2:
                        e1deps.append(cb.prev2[do - 2])
                    e1 = SOP((lambda e, d=pre[:, :], s=ps: e.activation(d, s[:, :], act)), e1deps)
                    bb.set_r(bank, e1)
                    rps = pBC1 if do % 2 == 0 else pBC2
                    rbank = BK["BC1"] if do % 2 == 0 else BK["BC2"]
                    rdeps = [e1] + bb.bw_deps(rbank)
                    rtok = None
                    for bbk in range(4):
                        rtok = bb.op("T", (lambda e, o=rps, pre=pre, bbk=bbk:
                                           e.matmul(o[:, 128 * bbk:128 * (bbk + 1)],
                                                    rope_sb[:, bbk, :],
                                                    pre[:, 128 * bbk:128 * (bbk + 1)],
                                                    start=True, stop=True)),
                                     rdeps if bbk == 0 else (), "T" if bbk == 3 else None)
                    bb.set_w(rbank, rtok)
                    ev = VOP((lambda e, d=dst_rot[:, do, :], s=rps: e.tensor_copy(d, s)), [rtok])
                    bb.set_r(rbank, ev)
                    cb.prev2[do] = rtok
                    return ev
                cb.prev2 = {}
                return cb

            q_toks = proj("wq", 8, xc_sb, rope_evac(qrot, AF.Silu), xc_all)
            k_toks = proj("wk", 8, xc_sb, rope_evac(krot, AF.Identity))

            def v_evac(do, mtok, ps, bank):
                ev = VOP((lambda e, do=do, s=ps: e.tensor_copy(v2only[:, do, :], s[:, :])), [mtok])
                bb.set_r(bank, ev)
                return ev

            v_toks = proj("wv", 16, xc_sb, v_evac)
            v2_sb = v2only

            def g_evac(do, mtok, ps, bank):
                ev = SOP((lambda e, do=do, s=ps: e.activation(g_sb[:, do, :], s[:, :], AF.Silu)),
                         [mtok])
                bb.set_r(bank, ev)
                return ev

            if dbg_p2 and t == 0:
                bb.dma("Y", dbg["dbg_qrot"][:, :, :], qrot[:, :, :], deps=q_toks, inc="y")
                bb.dma("Y", dbg["dbg_krot"][:, :, :], krot[:, :, :], deps=k_toks, inc="y")
                bb.dma("Y", dbg["dbg_v"][:, :, :], v2only[:, :, :], deps=v_toks, inc="y")
            rtokA = None
            d0 = list(v_toks) + [tok_rw] + bb.bw_deps(BK["R0"])
            for cc in range(16):
                rtokA = bb.op("T", (lambda e, cc=cc: e.matmul(pR0[:, :], rw_sb[:, 0, cc, :],
                                                              v2_sb[:, cc, :],
                                                              start=(cc == 0), stop=(cc == 15))),
                              d0 if cc == 0 else (), "T" if cc == 15 else None)
            bb.set_w(BK["R0"], rtokA)
            rtokB = None
            d1_ = bb.bw_deps(BK["R1"])
            for cc in range(16):
                rtokB = bb.op("T", (lambda e, cc=cc: e.matmul(pR1[:, :], rw_sb[:, 1, cc, :],
                                                              v2_sb[:, cc, :],
                                                              start=(cc == 0), stop=(cc == 15))),
                              d1_ if cc == 0 else (), "T" if cc == 15 else None)
            bb.set_w(BK["R1"], rtokB)
            rA = VOP((lambda e: e.tensor_copy(R["l0r"][:, :], pR0[:, :])), [rtokA])
            bb.set_r(BK["R0"], rA)
            rB = VOP((lambda e: e.tensor_copy(R["l1r"][:, :], pR1[:, :])), [rtokB])
            bb.set_r(BK["R1"], rB)

            h1 = VOP((lambda e: e.tensor_tensor(R["t0r"][:, :], R["l0r"][:, :], R["l1r"][:, :],
                                                ALU.subtract)), [rA, rB])
            h2 = SOP((lambda e: e.activation(R["s0r"][:, :], R["t0r"][:, :], AF.Sigmoid)), [h1])
            h3 = VOP((lambda e: e.tensor_tensor(R["i0r"][:, :], R["l0r"][:, :], R["l1r"][:, :],
                                                ALU.is_ge)), [rA, rB])
            h4 = VOP((lambda e: e.tensor_scalar(R["i1r"][:, :], R["i0r"][:, :], -1.0, 1.0,
                                                ALU.mult, ALU.add)), [h3])
            h5 = VOP((lambda e: e.tensor_tensor(R["ms0r"][:, :], R["s0r"][:, :], R["i0r"][:, :],
                                                ALU.mult)), [h2, h3])
            h6 = VOP((lambda e: e.tensor_scalar(R["t1r"][:, :], R["s0r"][:, :], -1.0, 1.0,
                                                ALU.mult, ALU.add)), [h2])
            h7 = VOP((lambda e: e.tensor_tensor(R["ms1r"][:, :], R["t1r"][:, :], R["i1r"][:, :],
                                                ALU.mult)), [h6, h4])
            cs0 = VOP((lambda e: e.tensor_copy(R["si0r"][:, 0:128], R["i0r"][:, 0:128])), [h3])
            cs1 = VOP((lambda e: e.tensor_copy(R["si1r"][:, 0:128], R["i1r"][:, 0:128])), [h4])
            for blk in range(1, 4):
                cs0 = VOP((lambda e, blk=blk: e.tensor_tensor(
                    R["si0r"][:, 128 * blk:128 * (blk + 1)],
                    R["si0r"][:, 128 * (blk - 1):128 * blk],
                    R["i0r"][:, 128 * blk:128 * (blk + 1)], ALU.add)), [cs0])
                cs1 = VOP((lambda e, blk=blk: e.tensor_tensor(
                    R["si1r"][:, 128 * blk:128 * (blk + 1)],
                    R["si1r"][:, 128 * (blk - 1):128 * blk],
                    R["i1r"][:, 128 * blk:128 * (blk + 1)], ALU.add)), [cs1])
            HKS = HDK ** (-0.5)
            n1 = VOP((lambda e: e.tensor_scalar(R["t0r"][:, :], R["si0r"][:, :], 1.0, HKS,
                                                ALU.min, ALU.mult)), [cs0])
            n2 = VOP((lambda e: e.tensor_scalar_max(R["t1r"][:, :], R["si0r"][:, :], 1.0)), [n1])
            n3 = VOP((lambda e: e.reciprocal(R["t2r"][:, :], R["t1r"][:, :])), [n2])
            n4 = VOP((lambda e: e.tensor_tensor(R["c0r"][:, :], R["ms0r"][:, :], R["t0r"][:, :],
                                                ALU.mult)), [n3, h5])
            n5 = VOP((lambda e: e.tensor_tensor(R["c0r"][:, :], R["c0r"][:, :], R["t2r"][:, :],
                                                ALU.mult)), [n4])
            p1_ = VOP((lambda e: e.tensor_scalar(R["t0r"][:, :], R["si1r"][:, :], 1.0, HKS,
                                                 ALU.min, ALU.mult)), [cs1, n5])
            p2_ = VOP((lambda e: e.tensor_scalar_max(R["t1r"][:, :], R["si1r"][:, :], 1.0)), [p1_])
            p3_ = VOP((lambda e: e.reciprocal(R["t2r"][:, :], R["t1r"][:, :])), [p2_])
            p4_ = VOP((lambda e: e.tensor_tensor(R["c1r"][:, :], R["ms1r"][:, :], R["t0r"][:, :],
                                                 ALU.mult)), [p3_, h7])
            row_tok = VOP((lambda e: e.tensor_tensor(R["c1r"][:, :], R["c1r"][:, :], R["t2r"][:, :],
                                                     ALU.mult)), [p4_])

            if dbg_p2 and t == 0:
                for ri, rnm in enumerate(["l0r", "l1r", "s0r", "i0r", "i1r", "ms0r",
                                          "ms1r", "si0r", "c0r", "c1r"]):
                    bb.dma("Y", dbg["dbg_rows"][:, ri, :], R[rnm][:, :], deps=[row_tok], inc="y")
            o_tok = {}
            prev_dtok = None
            prev_btok = None
            lm_pairs = [(l, m) for l in range(4) for m in range(l + 1)]

            def pair_seq(l, m):
                    nonlocal prev_dtok, prev_btok
                    ddeps = [row_tok] + q_toks + k_toks + bb.bw_deps(("s1a",))
                    dtok = None
                    ptoks = {}
                    for c8 in range(8):
                        pr = prodb[c8 % 2]
                        pdeps = []
                        if (c8 - 2) in ptoks:
                            pdeps.append(ptoks[c8 - 2])
                        elif prev_dtok is not None:
                            pdeps.append(prev_dtok)
                        pt = VOP((lambda e, pr=pr, c8=c8, l=l, m=m:
                                  e.tensor_tensor(pr[:, :], qrot[:, c8, 128 * l:128 * (l + 1)],
                                                  krot[:, c8, 128 * m:128 * (m + 1)], ALU.mult)),
                                 pdeps)
                        dtok = bb.op("T", (lambda e, pr=pr, c8=c8: e.matmul(
                            pDOT, dotsel_sb[:, c8, :], pr[:, :],
                            start=(c8 == 0), stop=(c8 == 7))),
                            (ddeps + [pt]) if c8 == 0 else [pt], "T")
                        ptoks[c8] = dtok
                    prev_dtok = dtok
                    bb.set_w(("s1a",), dtok)
                    w1 = VOP((lambda e, l=l, m=m: e.tensor_tensor(
                        wrow[:, :], R["c0r"][:, 128 * l:128 * (l + 1)],
                        R["i0r"][:, 128 * m:128 * (m + 1)], ALU.mult)),
                        [row_tok] + ([prev_btok] if prev_btok else []))
                    w2 = VOP((lambda e, l=l, m=m: e.tensor_tensor(
                        wrow2[:, :], R["c1r"][:, 128 * l:128 * (l + 1)],
                        R["i1r"][:, 128 * m:128 * (m + 1)], ALU.mult)), [w1])
                    w3 = VOP((lambda e: e.tensor_tensor(wrow[:, :], wrow[:, :], wrow2[:, :],
                                                        ALU.add)), [w2])
                    qk1 = VOP((lambda e: e.tensor_tensor(qkw_sb[:, :], pDOT, wrow[:, :],
                                                         ALU.mult)), [w3, dtok])
                    bb.set_r(("s1a",), qk1)
                    for q in range(4):
                        bps = pBC1 if q % 2 == 0 else pBC2
                        bbank = BK["BC1"] if q % 2 == 0 else BK["BC2"]
                        bdeps = [qk1] + bb.bw_deps(bbank)
                        btok = None
                        for j in range(4):
                            cc = 4 * q + j
                            btok = bb.op("T", (lambda e, o=bps, j=j, cc=cc: e.matmul(
                                o[:, 128 * j:128 * (j + 1)], sel16_sb[:, cc, :],
                                qkw_sb[:, :], start=True, stop=True)),
                                bdeps if j == 0 else (), "T" if j == 3 else None)
                        bb.set_w(bbank, btok)
                        bview = bps.rearrange("p (a c) -> p a c", a=4)
                        v2q = v2_sb[:, 4 * q:4 * q + 4, 128 * m:128 * (m + 1)]
                        b4q = big4[:, 4 * q:4 * q + 4, 128 * l:128 * (l + 1)]
                        vdeps = [btok] + ([v_toks[4 * q + 3]] if m == 0 and l == 0 else [])
                        if q in (1, 3) and hasattr(bb, "last_out2"):
                            vdeps.append(bb.last_out2[0])
                        if m == 0:
                            ot = VOP((lambda e, bv=bview, vq=v2q, oq=b4q:
                                      e.tensor_tensor(oq, bv, vq, ALU.mult)),
                                     vdeps + ([o_tok[(l, q)]] if (l, q) in o_tok else []))
                        else:
                            tq = tmpf[2] if q % 2 == 0 else tmpf[0]
                            tv = tq[:, :].rearrange("p (a c) -> p a c", a=4)
                            ot1 = VOP((lambda e, bv=bview, vq=v2q, tv=tv:
                                       e.tensor_tensor(tv, bv, vq, ALU.mult)), vdeps)
                            ot = bb.op("V", (lambda e, tv=tv, oq=b4q:
                                             e.tensor_tensor(oq, oq, tv, ALU.add)),
                                       [ot1, o_tok[(l, q)]], "V")
                        bb.set_r(bbank, ot)
                        o_tok[(l, q)] = ot
                        prev_btok = btok

            g_toks = []
            for do in range(16):
                wi, wt, wtok = wload("wg", do, wgate)
                psx = pPA if do % 2 == 0 else pPB
                bankx = BK["PA"] if do % 2 == 0 else BK["PB"]
                pairs_ = [(psx[:, :], wt[:, cc, :], xc_sb[:, cc, :]) for cc in range(16)]
                mtok = mm_group(bb, bankx, pairs_, [wtok])
                wbuf_free[wi] = mtok
                g_toks.append(g_evac(do, mtok, psx, bankx))
                if do < len(lm_pairs):
                    pair_seq(*lm_pairs[do])
            if dbg_p2 and t == 0:
                bb.dma("Y", dbg["dbg_g"][:, :, :], g_sb[:, :, :], deps=g_toks, inc="y")

            ojobs = {}
            ostat = {}

            def emit_statsO(cc, first, last):
                s1, s2, ob, sq = ojobs[cc]
                d1 = [s1] + (bb.bw_deps(BK["R0"]) if first else [])
                bb.op("T", (lambda e, ob=ob, cc=cc, st=first, sp=last: e.matmul(
                    pR0[:, :], sel128_sb[:, cc, :], ob[:, :], start=st, stop=sp)),
                    d1, "T" if last else None)
                d2 = [s2] + (bb.bw_deps(BK["R1"]) if first else [])
                tk = bb.op("T", (lambda e, sq=sq, cc=cc, st=first, sp=last: e.matmul(
                    pR1[:, :], sel128_sb[:, cc, :], sq[:, :], start=st, stop=sp)),
                    d2, "T")
                ostat[cc] = tk
                return tk

            for cc in range(16):
                q = cc // 4
                deps = [o_tok[(3, q)], o_tok[(2, q)], o_tok[(1, q)], o_tok[(0, q)]]
                ob = smb[cc % 2]
                sq = smb[2 + cc % 2]
                sdep = list(deps)
                if (cc - 2) in ostat:
                    sdep.append(ostat[cc - 2])
                s1 = SOP((lambda e, d=ob[:, :], cc=cc: e.activation(d, big4[:, cc, :],
                                                                    AF.Identity)), sdep)
                s2 = SOP((lambda e, d=sq[:, :], cc=cc: e.activation(d, big4[:, cc, :],
                                                                    AF.Square)), [s1])
                ojobs[cc] = (s1, s2, ob, sq)
                if cc >= 1:
                    emit_statsO(cc - 1, cc - 1 == 0, False)
            htok = emit_statsO(15, False, True)
            bb.set_w(BK["R0"], htok)
            bb.set_w(BK["R1"], htok)
            hr1 = VOP((lambda e: e.tensor_copy(hrow["sum"][:, :], pR0[:, :])), [htok])
            bb.set_r(BK["R0"], hr1)
            hr2 = VOP((lambda e: e.tensor_scalar_mul(hrow["rstd"][:, :], pR1[:, :], 1.0 / HDV)),
                      [htok])
            bb.set_r(BK["R1"], hr2)
            hr3 = VOP((lambda e: e.tensor_scalar_mul(hrow["mu"][:, :], hrow["sum"][:, :],
                                                     1.0 / HDV)), [hr1])
            hr4 = SOP((lambda e: e.activation(hrow["var"][:, :], hrow["mu"][:, :], AF.Square)),
                      [hr3])
            hr6 = VOP((lambda e: e.tensor_tensor(hrow["var"][:, :], hrow["rstd"][:, :],
                                                 hrow["var"][:, :], ALU.subtract)), [hr2, hr4])
            hr7 = SOP((lambda e: e.activation(hrow["var"][:, :], hrow["var"][:, :], AF.Ln,
                                              bias=ap_eps16)), [hr6, tok_cp])
            hr8 = SOP((lambda e: e.activation(hrow["rstd"][:, :], hrow["var"][:, :], AF.Exp,
                                              scale=ap_mhalf16)), [hr7])
            hb1 = SOP((lambda e: e.activation(murb[:, :], hrow["mu"][:, :], AF.Identity)), [hr3])
            hb2 = SOP((lambda e: e.activation(rsrb[:, :], hrow["rstd"][:, :], AF.Identity)), [hr8])
            go_toks = []
            for cc in range(16):
                bd1 = [hb1] + bb.bw_deps(BK["BC1"])
                tb1 = bb.op("T", (lambda e, cc=cc: e.matmul(pBC1[:, :], sel16_sb[:, cc, :],
                                                            murb[:, :], start=True, stop=True)),
                            bd1, "T")
                bb.set_w(BK["BC1"], tb1)
                bd2 = [hb2] + bb.bw_deps(BK["BC2"])
                tb2 = bb.op("T", (lambda e, cc=cc: e.matmul(pBC2[:, :], sel16_sb[:, cc, :],
                                                            rsrb[:, :], start=True, stop=True)),
                            bd2, "T")
                bb.set_w(BK["BC2"], tb2)
                g1 = VOP((lambda e, cc=cc: e.tensor_tensor(tmpf[0][:, :], big4[:, cc, :],
                                                           pBC1[:, :], ALU.subtract)), [tb1])
                bb.set_r(BK["BC1"], g1)
                g2 = VOP((lambda e, cc=cc: e.tensor_tensor(tmpf[1][:, :], tmpf[0][:, :],
                                                           pBC2[:, :], ALU.mult)), [g1, tb2])
                bb.set_r(BK["BC2"], g2)
                g3 = VOP((lambda e, cc=cc: e.tensor_tensor(g_sb[:, cc, :], tmpf[1][:, :],
                                                           g_sb[:, cc, :], ALU.mult)),
                         [g2, g_toks[cc]])
                go_toks.append(g3)
            go_sb = g_sb

            dbg_o_tok = None
            if dbg_p2 and t == 0:
                dbg_o_tok = bb.dma("Y", dbg["dbg_o"][:, :, :], big4[:, :, :],
                                   deps=[o_tok[(l_, q_)] for l_ in range(4) for q_ in range(4)],
                                   inc="y")
            fjobs = {}
            fstat = {}

            def emit_statsF(do, first, last):
                f2, f3, xb, sqb = fjobs[do]
                d1 = [f2] + (bb.bw_deps(BK["S1"]) if first else [])
                bb.op("T", (lambda e, xb=xb, st=first, sp=last: e.matmul(
                    pS1[:, :], onec_sb[:, :], xb[:, :], start=st, stop=sp)),
                    d1, "T" if last else None)
                d2 = [f3] + (bb.bw_deps(BK["S2"]) if first else [])
                tk = bb.op("T", (lambda e, sqb=sqb, st=first, sp=last: e.matmul(
                    pS2[:, :], onec_sb[:, :], sqb[:, :], start=st, stop=sp)),
                    d2, "T")
                fstat[do] = tk
                return tk

            for do in range(16):
                wi, wt, wtok = wload("wout", do, wgate)
                ps = pPA if do % 2 == 0 else pPB
                bank = BK["PA"] if do % 2 == 0 else BK["PB"]
                pairs = [(ps[:, :], wt[:, cc, :], go_sb[:, cc, :]) for cc in range(16)]
                mtok = mm_group(bb, bank, pairs, [wtok] + (go_toks if do == 0 else []))
                wbuf_free[wi] = mtok
                f1 = VOP((lambda e, do=do, s=ps: e.tensor_tensor(big4[:, do, :], s[:, :],
                                                                 u_sb[:, do, :], ALU.add)),
                         [mtok, u_toks[do]] + ([dbg_o_tok] if dbg_o_tok else []))
                bb.set_r(bank, f1)
                xb = smb[do % 2]
                sqb = smb[2 + do % 2]
                f2deps = [f1]
                if (do - 2) in fstat:
                    f2deps.append(fstat[do - 2])
                f2 = SOP((lambda e, d=xb[:, :], do=do: e.activation(d, big4[:, do, :],
                                                                    AF.Identity)), f2deps)
                f3 = SOP((lambda e, d=sqb[:, :], do=do: e.activation(d, big4[:, do, :],
                                                                     AF.Square)), [f2])
                fjobs[do] = (f2, f3, xb, sqb)
                if do >= 1:
                    emit_statsF(do - 1, do - 1 == 0, False)
            ftok = emit_statsF(15, False, True)
            bb.set_w(BK["S1"], ftok)
            bb.set_w(BK["S2"], ftok)
            fc1, fc2 = ln_rows(ftok, 2048.0)
            if not hasattr(bb, "out_store"):
                bb.out_store = {}
            for do in range(16):
                z1 = VOP((lambda e, do=do: e.tensor_tensor(tmpf[2][:, :], big4[:, do, :],
                                                           mu_bc[:, :], ALU.subtract)), [fc1, fc2])
                ob = outb[do % 2]
                z2deps = [z1]
                if (t, do - 2) in bb.out_store:
                    z2deps.append(bb.out_store[(t, do - 2)])
                elif (t - 1, do + 14) in bb.out_store:
                    z2deps.append(bb.out_store[(t - 1, do + 14)])
                z2 = VOP((lambda e, d=ob[:, :]: e.tensor_tensor(d, tmpf[2][:, :], rs_bc[:, :],
                                                                ALU.mult)), z2deps)
                bb.out_store[(t, do)] = bb.dma(
                    "G", out_own[do, :, 512 * t:512 * (t + 1)], ob[:, :], deps=[z2], inc="g")
            bb.last_out2 = [bb.out_store[(t, 14)], bb.out_store[(t, 15)]]
            bb.last_fin = z2
        ctx2.close()

    if do_p2:
        bb.w("G", ("g", bb.semcnt["g"]))
    else:
        bb.w("Y", ("y", bb.semcnt["y"]))

    with ExitStack() as rctx:
        sems = {n: rctx.enter_context(nc.semaphore(f"sem_{n}"))
                for n in ["T", "V", "S", "g", "y", "c", "P"]}
        block = rctx.enter_context(nc.Block())

        def mkreplay(key):
            def f(eng):
                for item in bb.prog[key]:
                    if item[0] == "w":
                        eng.wait_ge(sems[item[1]], item[2])
                    else:
                        ins = item[1](eng)
                        for (s, n) in item[2]:
                            ins.then_inc(sems[s], n)
            return f

        block.tensor(mkreplay("T"))
        block.vector(mkreplay("V"))
        block.scalar(mkreplay("S"))
        block.gpsimd(mkreplay("G"))
        block.sync(mkreplay("Y"))

    ctx_all.close()
    return nc


def host_constants():
    c_ident = np.eye(128, dtype=np.float32).astype(BFD)
    c_ones_col = np.ones((128, 1), np.float32).astype(BFD)
    c_ones_row = np.ones((1, 128), np.float32)
    masks = np.zeros((128, 4, 512), np.float32)
    kloc = np.arange(128)[:, None]
    qloc = np.arange(512)[None, :]
    for j in range(4):
        masks[:, j, :] = (qloc >= 128 * j + kloc).astype(np.float32)
    c_masks = masks.astype(BFD)
    inv_freq = 1.0 / (10000.0 ** (np.arange(0, HDK, 2, dtype=np.float64) / HDK))
    c_rope = np.zeros((128, 4, 128), np.float64)
    for l in range(4):
        cosv = np.cos(l * inv_freq)
        sinv = np.sin(l * inv_freq)
        Rl = np.zeros((64, 64))
        for i in range(32):
            Rl[i, i] = cosv[i]
            Rl[i, i + 32] = -sinv[i]
            Rl[i + 32, i + 32] = cosv[i]
            Rl[i + 32, i] = sinv[i]
        R2 = np.zeros((128, 128))
        R2[0:64, 0:64] = Rl
        R2[64:128, 64:128] = Rl
        c_rope[:, l, :] = R2.T
    c_rope = c_rope.astype(np.float32).astype(BFD)
    sel16 = np.zeros((16, 16, 128), np.float32)
    for cc in range(16):
        sel16[cc, cc, :] = 1.0
    sel128 = np.zeros((128, 16, 16), np.float32)
    for cc in range(16):
        sel128[:, cc, cc] = 1.0
    dotsel = np.zeros((128, 8, 16), np.float32)
    for c8 in range(8):
        dotsel[0:64, c8, 2 * c8] = 1.0
        dotsel[64:128, c8, 2 * c8 + 1] = 1.0
    return dict(c_ident=c_ident, c_ones_col=c_ones_col, c_ones_row=c_ones_row,
                c_masks=c_masks, c_rope=c_rope, c_sel16=sel16.astype(BFD),
                c_sel128=sel128.astype(BFD), c_dotsel=dotsel.astype(BFD))


def chunk_lhsT(M):
    din, dout = M.shape
    return np.ascontiguousarray(M.reshape(16, 128, dout // 128, 128).transpose(2, 1, 0, 3))


def host_phase1_inputs(x, wq, wk, wv):
    xT = np.ascontiguousarray(x.reshape(B, S, 16, 128).transpose(0, 3, 2, 1))
    percore = []
    for c in range(NCORE):
        wt = np.zeros((128, 3, LH, 16, 128), np.float32)
        for pi, W in enumerate([wq, wk, wv]):
            for lhh in range(LH):
                h = LH * c + lhh
                Wh = W[128 * h:128 * (h + 1), :]
                wt[:, pi, lhh, :, :] = Wh.T.reshape(16, 128, 128).transpose(1, 0, 2)
        percore.append(np.ascontiguousarray(wt))
    return xT, percore


def host_phase2_inputs(x, wo, bo, conv_w, Wq, Wk, Wv, Wg, router_w, Wout):
    shared = dict(
        w_woT=chunk_lhsT(np.ascontiguousarray(wo.T)),
        w_wq=chunk_lhsT(Wq), w_wk=chunk_lhsT(Wk), w_wv=chunk_lhsT(Wv),
        w_wg=chunk_lhsT(Wg), w_wout=chunk_lhsT(Wout))
    rwf = np.zeros((128, 2, 16, 16), np.float32)
    for cc in range(16):
        rwf[:, 0, cc, cc] = router_w[cc, :, 0]
        rwf[:, 1, cc, cc] = router_w[cc, :, 1]
    shared["rw_f"] = rwf
    shared["bo_f"] = np.ascontiguousarray(bo.reshape(16, 128).T.reshape(128, 16, 1))
    shared["cw0_f"] = np.ascontiguousarray(conv_w[:, 0].reshape(16, 128).T.reshape(128, 16, 1))
    shared["cw1_f"] = np.ascontiguousarray(conv_w[:, 1].reshape(16, 128).T.reshape(128, 16, 1))
    percore = []
    for c in range(NCORE):
        xo = x[:, SC * c:SC * (c + 1), :].reshape(B, NT, 128, 16, 128)
        percore.append(np.ascontiguousarray(
            xo.transpose(1, 4, 3, 0, 2).reshape(NT, 128, 16, 512)))
    return shared, percore


def assemble_output(results):
    out = np.empty((B, S, D), np.float32)
    for c in range(NCORE):
        oc = np.asarray(results[c]["out_own"], np.float32)
        oc = oc.reshape(16, 128, NT, B, 128).transpose(3, 2, 4, 0, 1)
        out[:, SC * c:SC * (c + 1), :] = oc.reshape(B, SC, D)
    return out


def kernel(**inputs):
    x = np.ascontiguousarray(np.asarray(inputs["L0_attn_x"], np.float32))
    consts = host_constants()
    xT, wqkvT_pc = host_phase1_inputs(
        x, np.asarray(inputs["L0_attn_wq"], np.float32),
        np.asarray(inputs["L0_attn_wk"], np.float32),
        np.asarray(inputs["L0_attn_wv"], np.float32))
    shared2, xown_pc = host_phase2_inputs(
        x, np.asarray(inputs["L0_attn_wo"], np.float32),
        np.asarray(inputs["L0_attn_bo"], np.float32),
        np.asarray(inputs["L0_ffn_conv_w"], np.float32),
        np.asarray(inputs["L0_ffn_Wq"], np.float32),
        np.asarray(inputs["L0_ffn_Wk"], np.float32),
        np.asarray(inputs["L0_ffn_Wv"], np.float32),
        np.asarray(inputs["L0_ffn_Wg"], np.float32),
        np.asarray(inputs["L0_ffn_router_w"], np.float32),
        np.asarray(inputs["L0_ffn_Wout"], np.float32))
    in_maps = []
    for c in range(NCORE):
        m = dict(consts)
        m.update(shared2)
        m["xT"] = xT
        m["wqkvT"] = wqkvT_pc[c]
        m["x_own"] = xown_pc[c]
        in_maps.append(m)
    nc = build_program(do_p1=True, do_p2=True)
    res = run_bass_kernel_spmd(nc, in_maps, core_ids=list(range(NCORE)))
    return assemble_output(res.results)
